# version 17
# baseline (speedup 1.0000x reference)
import math
import contextlib
import numpy as np
import ml_dtypes

import concourse.bass as bass
import concourse.mybir as mybir
from concourse.bass_utils import run_bass_kernel_spmd

F32 = mybir.dt.float32
BF16 = mybir.dt.bfloat16
AF = mybir.ActivationFunctionType
OP = mybir.AluOpType
AX = mybir.AxisListType

MAGIC = float(np.float32(1.5 * 2 ** 23))
TWO_PI = 2.0 * math.pi

N_CORES = 8
S = 16384
SC = S // N_CORES
D = 1024
NE = 16
NT = SC // 128
NDC = D // 128

ENGINES = ("sync", "gpsimd", "tensor", "vector", "scalar")

EXP_SLOT = [3 * (e // 2) + (e % 2) for e in range(NE)]
SLOT_WS0, SLOT_WS1, SLOT_WQ, SLOT_WK, SLOT_WV = 2, 5, 8, 11, 14


def build_nc():
    nc = bass.Bass("TRN2", target_bir_lowering=False)

    x_in = nc.dram_tensor("x", [SC, D], F32, kind="ExternalInput")
    wsrc = nc.dram_tensor("wsrc", [3, D, D], F32, kind="ExternalInput")
    bq = nc.dram_tensor("bq", [D], F32, kind="ExternalInput")
    bk = nc.dram_tensor("bk", [D], F32, kind="ExternalInput")
    bv = nc.dram_tensor("bv", [D], F32, kind="ExternalInput")
    phb = nc.dram_tensor("phb", [D], F32, kind="ExternalInput")
    cen = nc.dram_tensor("cen", [NE, D], F32, kind="ExternalInput")
    ebi = nc.dram_tensor("ebi", [NE], F32, kind="ExternalInput")
    brr = nc.dram_tensor("brr", [NE, D], F32, kind="ExternalInput")
    bs = nc.dram_tensor("bs", [2, D], F32, kind="ExternalInput")
    ident = nc.dram_tensor("ident", [128, 128], BF16, kind="ExternalInput")
    out = nc.dram_tensor("out", [SC, D], F32, kind="ExternalOutput")

    wtb_in = nc.dram_tensor("wtb_in", [3, NDC, 128, D], BF16)
    wtb_out = nc.dram_tensor("wtb_out", [24, NDC, 128, D], BF16, addr_space="Shared")
    kvch_in = [nc.dram_tensor(f"kvch_in{h}", [2, NDC, 128, SC // 2], BF16) for h in range(2)]
    kvch_out = [nc.dram_tensor(f"kvch_out{h}", [16, NDC, 128, SC // 2], BF16, addr_space="Shared")
                for h in range(2)]
    kvv_in = nc.dram_tensor("kvv_in", [NT, 128, D], BF16)
    kvv_out = nc.dram_tensor("kvv_out", [NT * 8, 128, D], BF16, addr_space="Shared")
    qsc = nc.dram_tensor("qsc", [2, NDC, 128, SC], BF16)
    x1sp = nc.dram_tensor("x1sp", [NT, 128, D], F32)
    x2sp = nc.dram_tensor("x2sp", [NT, 128, D], F32)

    xv = x_in.ap().rearrange("(t p) d -> t p d", p=128)
    outv = out.ap().rearrange("(t p) d -> t p d", p=128)

    Q = {e: [] for e in ENGINES}
    C = {"LD": 0, "GP": 0, "CC": 0, "TE": 0, "VE": 0, "SE": 0, "PH": 0}
    SEM = {}

    def op(eng, meth, *a, inc=None, amt=1, **kw):
        if inc is not None:
            C[inc] += amt

        def f(a=a, kw=kw, meth=meth, eng=eng, inc=inc, amt=amt):
            instr = getattr(getattr(nc, eng), meth)(*a, **kw)
            if inc is not None:
                instr.then_inc(SEM[inc], amt)
        Q[eng].append(f)
        return C[inc] if inc is not None else None

    def wait(eng, sem, val):
        if val is None or val <= 0:
            return
        Q[eng].append(lambda eng=eng, sem=sem, val=val: getattr(nc, eng).wait_ge(SEM[sem], val))

    def drain(eng="vector"):
        Q[eng].append(lambda eng=eng: getattr(nc, eng).drain())

    def collective(t_in, t_out):
        wait("gpsimd", "CC", C["CC"])
        op("gpsimd", "collective_compute", "AllGather", OP.bypass,
           replica_groups=[list(range(N_CORES))],
           ins=[t_in.ap().opt()], outs=[t_out.ap().opt()], inc="CC", amt=1)
        return C["CC"]

    def barrier():
        wait("sync", "LD", C["LD"])
        wait("gpsimd", "GP", C["GP"])
        C["PH"] += len(ENGINES)
        tgt = C["PH"]
        for e in ENGINES:
            Q[e].append(lambda e=e: getattr(nc, e).nop().then_inc(SEM["PH"], 1))
            wait(e, "PH", tgt)

    est = contextlib.ExitStack()
    for s in C:
        SEM[s] = est.enter_context(nc.semaphore(s))

    def sb(name, shape, dt):
        return est.enter_context(nc.sbuf_tensor(name, shape, dt))

    ident_sb = sb("ident_sb", [128, 128], BF16)
    ones_col = sb("ones_col", [128, 1], BF16)
    ones_row = sb("ones_row", [1, 128], F32)
    eps_col = sb("eps_col", [128, 1], F32)
    m32_col = sb("m32_col", [128, 1], F32)
    invwl = sb("invwl", [128, NDC], F32)
    scaleA = sb("scaleA", [128, NDC], F32)
    biasAk = sb("biasAk", [128, NDC], F32)
    biasAkC = sb("biasAkC", [128, NDC], F32)
    biasAq = sb("biasAq", [128, NDC], F32)
    biasAqC = sb("biasAqC", [128, NDC], F32)
    smallA = sb("smallA", [128, NDC], F32)
    smallB = sb("smallB", [128, NDC], F32)
    ct_sb = sb("ct_sb", [128, NDC * NE], BF16)
    br_bf = sb("br_bf", [NE, D], BF16)
    bs_row = sb("bs_row", [1, D], BF16)
    ones_rbf = sb("ones_rbf", [1, 128], BF16)
    eb_bc = sb("eb_bc", [128, NE], F32)
    bv_bc = sb("bv_bc", [128, D], BF16)
    wssum = sb("wssum", [128, NDC * D], BF16)
    st_sum = sb("st_sum", [128, NT], F32)
    st_ssq = sb("st_ssq", [128, NT], F32)
    st_mean = sb("st_mean", [128, NT], F32)
    st_ex2 = sb("st_ex2", [128, NT], F32)
    st_msq = sb("st_msq", [128, NT], F32)
    st_var = sb("st_var", [128, NT], F32)
    st_sd = sb("st_sd", [128, NT], F32)
    st_rstd = sb("st_rstd", [128, NT], F32)
    st_nmb = sb("st_nmb", [128, NT], F32)
    sq_scr = sb("sq_scr", [128, D], F32)
    g_s0 = sb("g_s0", [128, NT * NE], F32)
    g_exp = sb("g_exp", [128, NT * NE], F32)
    g_m1 = sb("g_m1", [128, NT * NE], F32)
    g_iv = sb("g_iv", [128, NT * NE], F32)
    g_t2 = g_s0
    g_m2 = g_iv
    g_ms = g_s0
    g_g = sb("g_g", [128, NT * NE], F32)
    g_bf = sb("g_bf", [128, NT * NE], BF16)
    gx_mx = sb("gx_mx", [128, NT], F32)
    gx_se = sb("gx_se", [128, NT], F32)
    gx_se2 = sb("gx_se2", [128, NT], F32)
    gx_rc = sb("gx_rc", [128, NT], F32)
    gx_m2 = sb("gx_m2", [128, NT], F32)
    gT = sb("gT", [NE, SC], BF16)

    def seg(t):
        return t[:, :].rearrange("p (t e) -> p t e", e=NE)

    def bcol(t):
        return t[:, :].unsqueeze(2).broadcast_to([128, NT, NE])

    ctx = {}

    scopeA1 = contextlib.ExitStack()
    ctx["uT"] = scopeA1.enter_context(nc.sbuf_tensor("uT", [128, NDC * SC], BF16))
    scopeA2 = contextlib.ExitStack()
    ctx["xbuf"] = scopeA2.enter_context(nc.sbuf_tensor("xbuf", [128, NT * D], F32))
    ctx["wr4"] = [scopeA2.enter_context(nc.sbuf_tensor(f"wr4_{i}", [128, NDC * D], BF16)) for i in range(3)]
    ctx["u_bf"] = [scopeA2.enter_context(nc.sbuf_tensor(f"u_bf_{i}", [128, D], BF16)) for i in range(2)]
    wbld = [scopeA2.enter_context(nc.sbuf_tensor(f"wbld_{i}", [128, D], BF16)) for i in range(2)]
    bs1_row = scopeA2.enter_context(nc.sbuf_tensor("bs1_row", [1, D], F32))
    eb_row = scopeA2.enter_context(nc.sbuf_tensor("eb_row", [1, NE], F32))
    bv_row = scopeA2.enter_context(nc.sbuf_tensor("bv_row", [1, D], F32))

    psA = contextlib.ExitStack()
    ctx["ps_pair"] = [psA.enter_context(nc.psum_tensor(f"ps_pair{i}", [128, D], F32)) for i in range(2)]
    psTR = contextlib.ExitStack()
    ctx["ps_tr"] = [psTR.enter_context(nc.psum_tensor(f"ps_tr{i}", [128, D], BF16)) for i in range(2)]
    ctx["ps_sc"] = psTR.enter_context(nc.psum_tensor("ps_sc", [128, 512], F32))
    psCD = contextlib.ExitStack()

    op("sync", "dma_start", out=ident_sb[:, :], in_=ident.ap(), inc="LD", amt=16)
    ld_ident = C["LD"]

    op("sync", "dma_start", out=eb_row[:, :], in_=ebi.ap().unsqueeze(0), inc="LD", amt=16)
    op("sync", "dma_start", out=smallA[:, :], in_=bk.ap().rearrange("(c p) -> p c", p=128), allow_slow_non_contiguous=True, inc="LD", amt=16)
    op("sync", "dma_start", out=smallB[:, :], in_=phb.ap().rearrange("(c p) -> p c", p=128), allow_slow_non_contiguous=True, inc="LD", amt=16)
    ld_small = C["LD"]
    op("sync", "dma_start", out=bv_row[:, :], in_=bs.ap()[0:1, :], inc="LD", amt=16)
    op("sync", "dma_start", out=bs1_row[:, :], in_=bs.ap()[1:2, :], inc="LD", amt=16)
    ld_bs = C["LD"]

    op("vector", "memset", ones_col[:, :], 1.0, inc="VE")
    op("vector", "memset", ones_row[:, :], 1.0, inc="VE")
    op("vector", "memset", ones_rbf[:, :], 1.0, inc="VE")
    op("vector", "memset", eps_col[:, :], 1e-5, inc="VE")
    op("vector", "memset", m32_col[:, :], -32.0, inc="VE")
    ve_ones = C["VE"]

    op("gpsimd", "iota", invwl[:, :], pattern=[[128, NDC]], base=1, channel_multiplier=1,
       allow_small_or_imprecise_dtypes=True, inc="GP", amt=1)
    gp_iota = C["GP"]
    wait("vector", "GP", gp_iota)
    op("vector", "tensor_scalar", out=invwl[:, :], in0=invwl[:, :], scalar1=TWO_PI / D, scalar2=1e-8,
       op0=OP.mult, op1=OP.add, inc="VE")
    drain()
    op("vector", "reciprocal", invwl[:, :], invwl[:, :], inc="VE")
    drain()
    op("vector", "tensor_scalar", out=scaleA[:, :], in0=invwl[:, :], scalar1=1.0 / TWO_PI, scalar2=None,
       op0=OP.mult, inc="VE")

    def emit_biasA(bsrc_tile, dst, dstC):
        op("vector", "tensor_tensor", out=dst[:, :], in0=bsrc_tile[:, :], in1=invwl[:, :], op=OP.mult, inc="VE")
        drain()
        op("vector", "tensor_tensor", out=dst[:, :], in0=dst[:, :], in1=smallB[:, :], op=OP.add, inc="VE")
        drain()
        op("vector", "tensor_scalar", out=dst[:, :], in0=dst[:, :], scalar1=1.0 / TWO_PI, scalar2=None,
           op0=OP.mult, inc="VE")
        drain()
        op("vector", "tensor_scalar", out=dstC[:, :], in0=dst[:, :], scalar1=0.25, scalar2=None,
           op0=OP.add, inc="VE")
        return C["VE"]

    wait("vector", "LD", ld_small)
    ve_biask = emit_biasA(smallA, biasAk, biasAkC)
    wait("sync", "VE", ve_biask)
    op("sync", "dma_start", out=smallA[:, :], in_=bq.ap().rearrange("(c p) -> p c", p=128), allow_slow_non_contiguous=True, inc="LD", amt=16)
    ld_bq = C["LD"]
    wait("vector", "LD", ld_bq)
    emit_biasA(smallA, biasAq, biasAqC)
    wait("vector", "LD", ld_bs)
    op("vector", "tensor_tensor", out=bs_row[:, :], in0=bv_row[:, :], in1=bs1_row[:, :], op=OP.add, inc="VE")
    ve_bsrow = C["VE"]


    ld_x = []
    for t in range(NT):
        op("sync", "dma_start", out=ctx["xbuf"][:, t * D:(t + 1) * D], in_=xv[t], inc="LD", amt=16)
        ld_x.append(C["LD"])

    te_slot_done = {}
    ve_bld = {}
    gp_bld = {}
    wi = 0
    for j in range(3):
        wb = ctx["wr4"][j % 2]
        if j >= 2:
            wait("gpsimd", "TE", te_slot_done[j - 2])
        op("gpsimd", "dma_start", out=wb[:, :].rearrange("p (c d) -> p c d", c=NDC),
           in_=wsrc.ap()[j].rearrange("(c p) d -> p c d", p=128), inc="GP", amt=16)
        gp_cast = C["GP"]
        wait("tensor", "GP", gp_cast)
        if j == 0:
            wait("tensor", "LD", ld_ident)
        dst, jj = wtb_in, j
        for dc in range(NDC):
            pt = ctx["ps_tr"][wi % 2]
            bld = wbld[wi % 2]
            if wi >= 2:
                wait("tensor", "VE", ve_bld[wi - 2])
            for c in range(NDC):
                op("tensor", "transpose", pt[:, c * 128:(c + 1) * 128],
                   wb[:, c * D + dc * 128: c * D + (dc + 1) * 128], ident_sb[:, :],
                   inc="TE" if c == NDC - 1 else None)
            te_tp = C["TE"]
            wait("vector", "TE", te_tp)
            if wi >= 2:
                wait("vector", "GP", gp_bld[wi - 2])
            op("vector", "tensor_copy", out=bld[:, :], in_=pt[:, :], inc="VE")
            ve_bld[wi] = C["VE"]
            wait("gpsimd", "VE", ve_bld[wi])
            op("gpsimd", "dma_start", out=dst.ap()[jj, dc], in_=bld[:, :], inc="GP", amt=16)
            gp_bld[wi] = C["GP"]
            wi += 1
        te_slot_done[j] = C["TE"]
    ve_tr_all = C["VE"]
    wait("gpsimd", "GP", C["GP"])
    cc_wtb = collective(wtb_in, wtb_out)

    op("gpsimd", "dma_start", out=br_bf[:, :], in_=cen.ap(), inc="GP", amt=16)
    gp_cen = C["GP"]
    wait("tensor", "GP", gp_cen)
    wait("tensor", "VE", ve_tr_all)
    for dc in range(NDC):
        op("tensor", "transpose", ctx["ps_tr"][0][:, dc * NE:(dc + 1) * NE],
           br_bf[:, dc * 128:(dc + 1) * 128], ident_sb[0:NE, 0:NE],
           inc="TE" if dc == NDC - 1 else None)
    te_ct = C["TE"]
    wait("vector", "TE", te_ct)
    op("vector", "tensor_copy", out=ct_sb[:, :], in_=ctx["ps_tr"][0][:, 0:NDC * NE], inc="VE")
    ve_ct = C["VE"]

    wait("gpsimd", "TE", te_ct)
    op("gpsimd", "dma_start", out=br_bf[:, :], in_=brr.ap(), inc="GP", amt=16)
    gp_br = C["GP"]

    wait("tensor", "LD", ld_bs)
    wait("tensor", "VE", ve_ones)
    op("tensor", "matmul", ctx["ps_pair"][0][:, 0:NE], ones_row[:, :], eb_row[:, :],
       start=True, stop=True, inc="TE")
    te_eb = C["TE"]
    wait("vector", "TE", te_eb)
    op("vector", "tensor_copy", out=eb_bc[:, :], in_=ctx["ps_pair"][0][:, 0:NE], inc="VE")
    ve_eb = C["VE"]
    wait("sync", "VE", ve_bsrow)
    op("sync", "dma_start", out=bs1_row[:, :], in_=bv.ap().unsqueeze(0), inc="LD", amt=16)
    ld_bv2 = C["LD"]
    wait("tensor", "LD", ld_bv2)
    op("tensor", "matmul", ctx["ps_pair"][1][:, 0:512], ones_row[:, :], bs1_row[:, 0:512], start=True, stop=True)
    op("tensor", "matmul", ctx["ps_pair"][1][:, 512:1024], ones_row[:, :], bs1_row[:, 512:1024],
       start=True, stop=True, inc="TE")
    te_bv = C["TE"]
    wait("vector", "TE", te_bv)
    op("vector", "tensor_copy", out=bv_bc[:, :], in_=ctx["ps_pair"][1][:, :], inc="VE")
    ve_bv = C["VE"]

    wait("sync", "CC", cc_wtb)
    wait("sync", "TE", te_slot_done[2])
    op("sync", "dma_start", out=ctx["wr4"][0][:, :].rearrange("p (c d) -> p c d", c=NDC),
       in_=wtb_out.ap()[SLOT_WS0].rearrange("c p d -> p c d"), inc="LD", amt=16)
    op("sync", "dma_start", out=ctx["wr4"][1][:, :].rearrange("p (c d) -> p c d", c=NDC),
       in_=wtb_out.ap()[SLOT_WS1].rearrange("c p d -> p c d"), inc="LD", amt=16)
    ld_ws = C["LD"]
    wait("vector", "LD", ld_ws)
    op("vector", "tensor_tensor", out=wssum[:, :], in0=ctx["wr4"][0][:, :], in1=ctx["wr4"][1][:, :],
       op=OP.add, inc="VE")
    ve_wss = C["VE"]

    def emit_ln(add_half_u, spill_to=None):
        xb = ctx["xbuf"]
        uT = ctx["uT"]
        for t in range(NT):
            xt = xb[:, t * D:(t + 1) * D]
            op("scalar", "activation", out=sq_scr[:, :], in_=xt, func=AF.Square,
               accum_out=st_ssq[:, t:t + 1], inc="SE")
            op("vector", "tensor_reduce", out=st_sum[:, t:t + 1], in_=xt, axis=AX.X, op=OP.add, inc="VE")
        se_sq = C["SE"]
        drain()
        wait("vector", "SE", se_sq)
        op("vector", "tensor_scalar", out=st_mean[:, :], in0=st_sum[:, :], scalar1=1.0 / D, scalar2=None,
           op0=OP.mult, inc="VE")
        op("vector", "tensor_scalar", out=st_ex2[:, :], in0=st_ssq[:, :], scalar1=1.0 / D, scalar2=None,
           op0=OP.mult, inc="VE")
        op("vector", "tensor_tensor", out=st_msq[:, :], in0=st_mean[:, :], in1=st_mean[:, :], op=OP.mult, inc="VE")
        drain()
        op("vector", "tensor_tensor", out=st_var[:, :], in0=st_ex2[:, :], in1=st_msq[:, :], op=OP.subtract, inc="VE")
        ve_var = C["VE"]
        wait("scalar", "VE", ve_var)
        op("scalar", "activation", out=st_sd[:, :], in_=st_var[:, :], func=AF.Sqrt, bias=eps_col[:, 0:1], inc="SE")
        se_sd = C["SE"]
        wait("vector", "SE", se_sd)
        op("vector", "reciprocal", st_rstd[:, :], st_sd[:, :], inc="VE")
        drain()
        op("vector", "scalar_tensor_tensor", out=st_nmb[:, :], in0=st_mean[:, :], scalar=-1.0,
           in1=st_rstd[:, :], op0=OP.mult, op1=OP.mult, inc="VE")
        drain()
        ve_stats = C["VE"]

        ve_after = {}
        te_after = {}
        ve_u = {}
        for t in range(NT):
            xt = xb[:, t * D:(t + 1) * D]
            ub = ctx["u_bf"][t % 2]
            pt = ctx["ps_tr"][t % 2]
            if t >= 2:
                wait("vector", "TE", te_after[t - 2])
            op("vector", "tensor_scalar", out=ub[:, :], in0=xt, scalar1=st_rstd[:, t:t + 1],
               scalar2=st_nmb[:, t:t + 1], op0=OP.mult, op1=OP.add, inc="VE")
            ve_u[t] = C["VE"]
            if add_half_u:
                drain()
                op("vector", "scalar_tensor_tensor", out=xt, in0=ub[:, :], scalar=0.5, in1=xt,
                   op0=OP.mult, op1=OP.add, inc="VE")
            wait("tensor", "VE", ve_u[t])
            if t >= 2:
                wait("tensor", "VE", ve_after[t - 2])
            for dc in range(NDC):
                op("tensor", "transpose", pt[:, dc * 128:(dc + 1) * 128],
                   ub[:, dc * 128:(dc + 1) * 128], ident_sb[:, :],
                   inc="TE" if dc == NDC - 1 else None)
            te_after[t] = C["TE"]
            wait("vector", "TE", te_after[t])
            op("vector", "tensor_copy",
               out=uT[:, :].rearrange("p (c s) -> p c s", c=NDC)[:, :, t * 128:(t + 1) * 128],
               in_=pt[:, :].rearrange("p (c s) -> p c s", c=NDC), inc="VE")
            ve_after[t] = C["VE"]
            if spill_to is not None:
                wait("gpsimd", "VE", ve_u[t])
                op("gpsimd", "dma_start", out=spill_to.ap()[t], in_=xt, inc="GP", amt=16)
        return ve_after[NT - 1]

    def emit_ffn(ln_ve, init_pair_free, wr_init_free, out_dma, ps_trsc_stack, ps_cd_stack):
        xb = ctx["xbuf"]
        uT = ctx["uT"]
        wr4 = ctx["wr4"]
        ps_pair = ctx["ps_pair"]
        ps_sc = ctx["ps_sc"]
        unit = 0
        ve_evac = {}

        def pair_wait(u):
            if u >= 2:
                wait("tensor", "VE", ve_evac[u - 2])
            elif init_pair_free[u]:
                wait("tensor", "VE", init_pair_free[u])

        wait("tensor", "VE", ln_ve)
        wait("tensor", "VE", ve_ct)
        for t in range(NT):
            for dc in range(NDC):
                lhs = uT[:, dc * SC + t * 128: dc * SC + (t + 1) * 128]
                op("tensor", "matmul", ps_sc[:, t * NE:(t + 1) * NE], lhs, ct_sb[:, dc * NE:(dc + 1) * NE],
                   start=(dc == 0), stop=(dc == NDC - 1),
                   inc="TE" if (dc == NDC - 1 and t == NT - 1) else None)
        te_scores = C["TE"]

        wait("vector", "TE", te_scores)
        op("vector", "tensor_tensor", out=seg(g_s0),
           in0=ps_sc[:, 0:NT * NE].rearrange("p (t e) -> p t e", e=NE),
           in1=eb_bc[:, :].unsqueeze(1).broadcast_to([128, NT, NE]), op=OP.add, inc="VE")
        drain()
        op("vector", "tensor_reduce", out=gx_mx[:, :], in_=seg(g_s0), axis=AX.X, op=OP.max, inc="VE")
        drain()
        op("vector", "tensor_tensor", out=seg(g_s0), in0=seg(g_s0), in1=bcol(gx_mx), op=OP.subtract, inc="VE")
        ve_g1 = C["VE"]
        wait("scalar", "VE", ve_g1)
        op("scalar", "activation", out=g_exp[:, :], in_=g_s0[:, :], func=AF.Exp, inc="SE")
        se_gexp = C["SE"]
        wait("vector", "SE", se_gexp)
        op("vector", "tensor_reduce", out=gx_se[:, :], in_=seg(g_exp), axis=AX.X, op=OP.add, inc="VE")
        drain()
        op("vector", "tensor_scalar", out=gx_se2[:, :], in0=gx_se[:, :], scalar1=2.0, scalar2=None,
           op0=OP.mult, inc="VE")
        op("vector", "tensor_scalar", out=g_m1[:, :], in0=g_exp[:, :], scalar1=1.0, scalar2=None,
           op0=OP.is_ge, inc="VE")
        op("vector", "reciprocal", gx_rc[:, :], gx_se2[:, :], inc="VE")
        op("vector", "tensor_scalar", out=g_iv[:, :], in0=g_m1[:, :], scalar1=-1.0, scalar2=1.0,
           op0=OP.mult, op1=OP.add, inc="VE")
        drain()
        op("vector", "tensor_tensor", out=g_t2[:, :], in0=g_exp[:, :], in1=g_iv[:, :], op=OP.mult, inc="VE")
        drain()
        op("vector", "tensor_reduce", out=gx_m2[:, :], in_=seg(g_t2), axis=AX.X, op=OP.max, inc="VE")
        drain()
        op("vector", "tensor_tensor", out=seg(g_m2), in0=seg(g_t2), in1=bcol(gx_m2), op=OP.is_ge, inc="VE")
        drain()
        op("vector", "tensor_tensor", out=g_ms[:, :], in0=g_m1[:, :], in1=g_m2[:, :], op=OP.add, inc="VE")
        drain()
        op("vector", "tensor_tensor", out=g_iv[:, :], in0=g_exp[:, :], in1=g_ms[:, :], op=OP.mult, inc="VE")
        drain()
        op("vector", "tensor_tensor", out=seg(g_g), in0=seg(g_iv), in1=bcol(gx_rc), op=OP.mult, inc="VE")
        drain()
        op("vector", "tensor_copy", out=g_bf[:, :], in_=g_g[:, :], inc="VE")
        ve_gbf = C["VE"]

        wait("tensor", "VE", ve_gbf)
        ve_gT = None
        ve_gt_cp = {}
        for t in range(NT):
            pt = ctx["ps_tr"][t % 2]
            if t >= 2:
                wait("tensor", "VE", ve_gt_cp[t - 2])
            op("tensor", "transpose", pt[0:NE, 0:128], g_bf[:, t * NE:(t + 1) * NE], ident_sb[:, :], inc="TE")
            te_g = C["TE"]
            wait("vector", "TE", te_g)
            op("vector", "tensor_copy", out=gT[:, t * 128:(t + 1) * 128], in_=pt[0:NE, 0:128], inc="VE")
            ve_gt_cp[t] = ve_gT = C["VE"]

        ps_trsc_stack.close()
        pairs4 = ps_pair + [ps_cd_stack.enter_context(
            nc.psum_tensor(f"ps_cd{i}_{1 if out_dma else 0}", [128, D], F32)) for i in range(2)]

        wait("tensor", "VE", ve_gT)
        wait("tensor", "GP", gp_br)
        for t in range(NT):
            pp = ps_pair[unit % 2]
            pair_wait(unit)
            lhs = gT[:, t * 128:(t + 1) * 128]
            op("tensor", "matmul", pp[:, 0:512], lhs, br_bf[:, 0:512], start=True, stop=True)
            op("tensor", "matmul", pp[:, 512:1024], lhs, br_bf[:, 512:1024], start=True, stop=True, inc="TE")
            te_u = C["TE"]
            wait("vector", "TE", te_u)
            op("vector", "tensor_tensor", out=xb[:, t * D:(t + 1) * D], in0=pp[:, :],
               in1=xb[:, t * D:(t + 1) * D], op=OP.add, inc="VE")
            ve_evac[unit] = C["VE"]
            unit += 1

        ld_done = {}

        def emit_load(e, free_ve):
            if free_ve:
                wait("sync", "VE", free_ve)
            op("sync", "dma_start", out=wr4[e % 2][:, :].rearrange("p (c d) -> p c d", c=NDC),
               in_=wtb_out.ap()[EXP_SLOT[e]].rearrange("c p d -> p c d"), inc="LD", amt=16)
            ld_done[e] = C["LD"]

        wait("sync", "CC", cc_wtb)
        emit_load(0, wr_init_free[0])
        emit_load(1, wr_init_free[1])
        ve_pair_evac = {}
        unit0 = unit
        for e in range(NE):
            wait("tensor", "LD", ld_done[e])
            wb = wr4[e % 2]
            for t in range(NT):
                pp = pairs4[unit % 4]
                if unit >= unit0 + 4:
                    wait("tensor", "VE", ve_evac[unit - 4])
                elif unit == unit0:
                    wait("tensor", "VE", ve_evac[unit0 - 2])
                elif unit == unit0 + 1:
                    wait("tensor", "VE", ve_evac[unit0 - 1])
                for dc in range(NDC):
                    lhs = uT[:, dc * SC + t * 128: dc * SC + (t + 1) * 128]
                    op("tensor", "matmul", pp[:, 0:512], lhs, wb[:, dc * D: dc * D + 512],
                       start=(dc == 0), stop=(dc == NDC - 1))
                    op("tensor", "matmul", pp[:, 512:1024], lhs, wb[:, dc * D + 512:(dc + 1) * D],
                       start=(dc == 0), stop=(dc == NDC - 1),
                       inc="TE" if dc == NDC - 1 else None)
                te_u = C["TE"]
                wait("vector", "TE", te_u)
                op("vector", "scalar_tensor_tensor", out=xb[:, t * D:(t + 1) * D], in0=pp[:, :],
                   scalar=g_g[:, t * NE + e: t * NE + e + 1], in1=xb[:, t * D:(t + 1) * D],
                   op0=OP.mult, op1=OP.add, inc="VE")
                ve_evac[unit] = C["VE"]
                unit += 1
            ve_pair_evac[(NE // 2 - 1, NT - 2)] = ve_evac[unit - 2]
            ve_pair_evac[(NE // 2 - 1, NT - 1)] = ve_evac[unit - 1]
            if e + 2 < NE:
                emit_load(e + 2, ve_evac[unit - 1])

        wait("tensor", "VE", ve_wss)
        unit2 = 0
        ve_sh = {}
        for t in range(NT):
            pp = ps_pair[unit2 % 2]
            key = (NE // 2 - 1, t) if unit2 < 2 else None
            if unit2 >= 2:
                wait("tensor", "VE", ve_sh[unit2 - 2])
            else:
                wait("tensor", "VE", ve_pair_evac[(NE // 2 - 1, NT - 2 + unit2)])
            op("tensor", "matmul", pp[:, 0:512], ones_rbf[:, :], bs_row[:, 0:512], start=True, stop=False)
            op("tensor", "matmul", pp[:, 512:1024], ones_rbf[:, :], bs_row[:, 512:1024], start=True, stop=False)
            for dc in range(NDC):
                lhs = uT[:, dc * SC + t * 128: dc * SC + (t + 1) * 128]
                op("tensor", "matmul", pp[:, 0:512], lhs, wssum[:, dc * D: dc * D + 512],
                   start=False, stop=(dc == NDC - 1))
                op("tensor", "matmul", pp[:, 512:1024], lhs, wssum[:, dc * D + 512: (dc + 1) * D],
                   start=False, stop=(dc == NDC - 1),
                   inc="TE" if dc == NDC - 1 else None)
            te_u = C["TE"]
            wait("vector", "TE", te_u)
            op("vector", "scalar_tensor_tensor", out=xb[:, t * D:(t + 1) * D], in0=pp[:, :],
               scalar=0.25, in1=xb[:, t * D:(t + 1) * D], op0=OP.mult, op1=OP.add, inc="VE")
            ve_sh[unit2] = C["VE"]
            if out_dma:
                wait("sync", "VE", ve_sh[unit2])
                op("sync", "dma_start", out=outv[t], in_=xb[:, t * D:(t + 1) * D], inc="LD", amt=16)
            unit2 += 1
        return ve_sh[unit2 - 1]

    wait("scalar", "LD", ld_x[-1])
    wait("vector", "LD", ld_x[-1])
    emit_ln(add_half_u=True)
    ln1_ve = C["VE"]
    ffn1_ve = emit_ffn(ln1_ve, [ve_eb, ve_bv], [ve_wss, 0], out_dma=False,
                       ps_trsc_stack=psTR, ps_cd_stack=psCD)

    psCD.close()
    psTR2 = contextlib.ExitStack()
    ctx["ps_tr"] = [psTR2.enter_context(nc.psum_tensor(f"ps_tr2_{i}", [128, D], BF16)) for i in range(2)]
    wait("scalar", "VE", ffn1_ve)
    wait("gpsimd", "VE", ffn1_ve)
    emit_ln(add_half_u=False, spill_to=x1sp)
    ln2_ve = C["VE"]

    psTR2.close()
    psA.close()
    scopeA2.close()
    barrier()

    scopeB = contextlib.ExitStack()
    wqkv = scopeB.enter_context(nc.sbuf_tensor("wqkv", [128, NDC * D], BF16))
    csk_c = [scopeB.enter_context(nc.sbuf_tensor(f"csk_c{i}", [128, SC], BF16)) for i in range(2)]
    csk_s = [scopeB.enter_context(nc.sbuf_tensor(f"csk_s{i}", [128, SC], BF16)) for i in range(2)]
    tf_s = [scopeB.enter_context(nc.sbuf_tensor(f"tf_s{i}", [128, 512], F32)) for i in range(2)]
    tf_c = [scopeB.enter_context(nc.sbuf_tensor(f"tf_c{i}", [128, 512], F32)) for i in range(2)]
    tk_s = scopeB.enter_context(nc.sbuf_tensor("tk_s", [128, 512], F32))
    tk_c = scopeB.enter_context(nc.sbuf_tensor("tk_c", [128, 512], F32))
    v_bf = [scopeB.enter_context(nc.sbuf_tensor(f"v_bf{i}", [128, D], BF16)) for i in range(2)]
    psC = contextlib.ExitStack()
    ps_k = [psC.enter_context(nc.psum_tensor(f"ps_k{i}", [128, 512], F32)) for i in range(4)]
    ps_v = [psC.enter_context(nc.psum_tensor(f"ps_v{i}", [128, D], F32)) for i in range(2)]

    def emit_trig_pass(slot, bA, bAC, dst, half_done=None):
        uT = ctx["uT"]
        op("sync", "dma_start", out=wqkv[:, :].rearrange("p (c d) -> p c d", c=NDC),
           in_=wtb_out.ap()[slot].rearrange("c p d -> p c d"), inc="LD", amt=16)
        ld_w = C["LD"]
        wait("tensor", "LD", ld_w)
        wait("tensor", "VE", C["VE"])
        se_psum = {}
        ve_f = {}
        se_sin = {}
        gp_out = {}
        u2 = 0
        for sbk in range(4):
            for ec in range(NDC):
                pk = ps_k[u2 % 4]
                st_c = csk_c[0][:, (u2 % 4) * 512:(u2 % 4 + 1) * 512]
                st_s = csk_s[0][:, (u2 % 4) * 512:(u2 % 4 + 1) * 512]
                if u2 >= 4:
                    wait("tensor", "VE", se_psum[u2 - 4])
                for dc in range(NDC):
                    op("tensor", "matmul", pk[:, :],
                       wqkv[:, dc * D + ec * 128: dc * D + (ec + 1) * 128],
                       uT[:, dc * SC + sbk * 512: dc * SC + (sbk + 1) * 512],
                       start=(dc == 0), stop=(dc == NDC - 1),
                       inc="TE" if dc == NDC - 1 else None)
                te_k = C["TE"]
                wait("vector", "TE", te_k)
                if u2 >= 2:
                    wait("vector", "SE", se_sin[u2 - 2])
                op("vector", "tensor_scalar", out=tf_s[u2 % 2][:, :], in0=pk[:, :],
                   scalar1=scaleA[:, ec:ec + 1], scalar2=bA[:, ec:ec + 1],
                   op0=OP.mult, op1=OP.add, inc="VE")
                op("vector", "tensor_scalar", out=tf_c[u2 % 2][:, :], in0=pk[:, :],
                   scalar1=scaleA[:, ec:ec + 1], scalar2=bAC[:, ec:ec + 1],
                   op0=OP.mult, op1=OP.add, inc="VE")
                se_psum[u2] = C["VE"]
                op("vector", "tensor_scalar", out=tk_s[:, :], in0=tf_s[u2 % 2][:, :], scalar1=MAGIC,
                   scalar2=MAGIC, op0=OP.add, op1=OP.subtract, inc="VE")
                op("vector", "tensor_scalar", out=tk_c[:, :], in0=tf_c[u2 % 2][:, :], scalar1=MAGIC,
                   scalar2=MAGIC, op0=OP.add, op1=OP.subtract, inc="VE")
                op("vector", "tensor_tensor", out=tf_s[u2 % 2][:, :], in0=tf_s[u2 % 2][:, :],
                   in1=tk_s[:, :], op=OP.subtract, inc="VE")
                op("vector", "tensor_tensor", out=tf_c[u2 % 2][:, :], in0=tf_c[u2 % 2][:, :],
                   in1=tk_c[:, :], op=OP.subtract, inc="VE")
                ve_f[u2] = C["VE"]
                wait("scalar", "VE", ve_f[u2])
                if u2 >= 4:
                    wait("scalar", "LD", gp_out[u2 - 4])
                op("scalar", "activation", out=st_s, in_=tf_s[u2 % 2][:, :],
                   func=AF.Sin, scale=TWO_PI, inc="SE")
                op("scalar", "activation", out=st_c, in_=tf_c[u2 % 2][:, :],
                   func=AF.Sin, scale=TWO_PI, inc="SE")
                se_sin[u2] = C["SE"]
                wait("sync", "SE", se_sin[u2])
                op("sync", "dma_start", out=dst(0, ec, sbk), in_=st_c, inc="LD", amt=16)
                op("sync", "dma_start", out=dst(1, ec, sbk), in_=st_s, inc="LD", amt=16)
                gp_out[u2] = C["LD"]
                u2 += 1
            if half_done is not None and sbk % 2 == 1:
                wait("gpsimd", "LD", C["LD"])
                half_done(sbk // 2)
        return C["TE"]

    op("sync", "dma_start", out=wqkv[:, :].rearrange("p (c d) -> p c d", c=NDC),
       in_=wtb_out.ap()[SLOT_WV].rearrange("c p d -> p c d"), inc="LD", amt=16)
    ld_wv = C["LD"]
    wait("tensor", "LD", ld_wv)
    wait("tensor", "VE", ln2_ve)
    ve_v = {}
    gp_v = {}
    for t in range(NT):
        pv = ps_v[t % 2]
        if t >= 2:
            wait("tensor", "VE", ve_v[t - 2])
        uT = ctx["uT"]
        for dc in range(NDC):
            lhs = uT[:, dc * SC + t * 128: dc * SC + (t + 1) * 128]
            op("tensor", "matmul", pv[:, 0:512], lhs, wqkv[:, dc * D: dc * D + 512],
               start=(dc == 0), stop=(dc == NDC - 1))
            op("tensor", "matmul", pv[:, 512:1024], lhs, wqkv[:, dc * D + 512:(dc + 1) * D],
               start=(dc == 0), stop=(dc == NDC - 1),
               inc="TE" if dc == NDC - 1 else None)
        te_v = C["TE"]
        wait("vector", "TE", te_v)
        if t >= 2:
            wait("vector", "LD", gp_v[t - 2])
        op("vector", "tensor_tensor", out=v_bf[t % 2][:, :], in0=pv[:, :], in1=bv_bc[:, :], op=OP.add, inc="VE")
        ve_v[t] = C["VE"]
        wait("sync", "VE", ve_v[t])
        op("sync", "dma_start", out=kvv_in.ap()[t], in_=v_bf[t % 2][:, :], inc="LD", amt=16)
        gp_v[t] = C["LD"]

    wait("gpsimd", "LD", C["LD"])
    cc_kvv = collective(kvv_in, kvv_out)

    cc_half = {}

    def k_half_done(h):
        cc_half[h] = collective(kvch_in[h], kvch_out[h])

    def k_dst(region, ec, sbk):
        return kvch_in[sbk // 2].ap()[region, ec, :, (sbk % 2) * 512:(sbk % 2 + 1) * 512]

    wait("sync", "TE", C["TE"])
    emit_trig_pass(SLOT_WK, biasAk, biasAkC, k_dst, half_done=k_half_done)

    wait("sync", "TE", C["TE"])
    wait("scalar", "GP", C["GP"])
    emit_trig_pass(SLOT_WQ, biasAq, biasAqC,
                   lambda region, ec, sbk: qsc.ap()[region, ec, :, sbk * 512:(sbk + 1) * 512])

    psC.close()
    scopeB.close()
    scopeA1.close()
    barrier()

    scopeC = contextlib.ExitStack()
    chq_c = scopeC.enter_context(nc.sbuf_tensor("chq_c", [128, NDC * 1024], BF16))
    chq_s = scopeC.enter_context(nc.sbuf_tensor("chq_s", [128, NDC * 1024], BF16))
    ck_c = [scopeC.enter_context(nc.sbuf_tensor(f"ck_c{i}", [128, NDC * 512], BF16)) for i in range(2)]
    ck_s = [scopeC.enter_context(nc.sbuf_tensor(f"ck_s{i}", [128, NDC * 512], BF16)) for i in range(2)]
    ck_v = [scopeC.enter_context(nc.sbuf_tensor(f"ck_v{i}", [128, 4 * D], BF16)) for i in range(2)]
    accum = scopeC.enter_context(nc.sbuf_tensor("accum", [128, 8 * D], F32))
    den_acc = scopeC.enter_context(nc.sbuf_tensor("den_acc", [128, 8], F32))
    den_rc = scopeC.enter_context(nc.sbuf_tensor("den_rc", [128, 8], F32))
    p_bf = [scopeC.enter_context(nc.sbuf_tensor(f"p_bf{i}", [128, 256], BF16)) for i in range(2)]
    x1t = [scopeC.enter_context(nc.sbuf_tensor(f"x1t{i}", [128, D], F32)) for i in range(2)]
    x2t = [scopeC.enter_context(nc.sbuf_tensor(f"x2t{i}", [128, D], F32)) for i in range(2)]
    psD = contextlib.ExitStack()
    ps_sim = [psD.enter_context(nc.psum_tensor(f"ps_sim{i}", [128, 512], F32)) for i in range(2)]
    ps_out = [psD.enter_context(nc.psum_tensor(f"ps_out{i}", [128, D], F32)) for i in range(2)]
    ps_den = psD.enter_context(nc.psum_tensor("ps_den", [128, 512], F32))

    NCK = 32
    KT_PER = 4

    sim_snap = {}
    exp_snap = {}
    pv_snap = {}
    unit = 0
    pending_pv = None
    te_after_ck = {}
    gp_x2 = {}
    ve_x2_map = {}

    def emit_pv(u, qb, kt, ck, first, last, ve_free):
        if first and ve_free:
            wait("tensor", "VE", ve_free)
        wait("tensor", "SE", exp_snap[u])
        pb = p_bf[u % 2]
        vv = ck_v[ck % 2]
        for sub in range(2):
            po = ps_out[sub]
            lhs = pb[:, sub * 128:(sub + 1) * 128]
            op("tensor", "matmul", po[:, 0:512], lhs, vv[:, kt * D: kt * D + 512],
               start=first, stop=last)
            op("tensor", "matmul", po[:, 512:1024], lhs, vv[:, kt * D + 512:(kt + 1) * D],
               start=first, stop=last)
            op("tensor", "matmul", ps_den[:, sub:sub + 1], lhs, ones_col[:, :],
               start=(first and sub == 0), stop=last, inc="TE" if sub == 1 else None)
        pv_snap[u] = C["TE"]

    for h in range(2):
        op("vector", "memset", accum[:, :], 0.0, inc="VE")
        op("vector", "memset", den_acc[:, :], 0.0, inc="VE")
        ve_zero = C["VE"]
        if h > 0:
            wait("sync", "TE", te_after_ck[(h - 1, NCK - 1)])
        op("sync", "dma_start", out=chq_c[:, :].rearrange("p (c f) -> p c f", c=NDC),
           in_=qsc.ap()[0, :, :, h * 1024:(h + 1) * 1024], inc="LD", amt=16)
        op("sync", "dma_start", out=chq_s[:, :].rearrange("p (c f) -> p c f", c=NDC),
           in_=qsc.ap()[1, :, :, h * 1024:(h + 1) * 1024], inc="LD", amt=16)
        ld_q = C["LD"]
        ve_evac_q = {qb: ve_zero for qb in range(4)}
        ck_order = [(t0i, r) for t0i in range(4) for r in range(8)]
        for ck in range(NCK):
            t0i, r = ck_order[ck]
            t0 = t0i * 512
            h2, t0h = t0 // 1024, t0 % 1024
            if ck == 0:
                wait("sync", "CC", max(cc_kvv, cc_half[0]))
            if t0i == 2 and r == 0:
                wait("sync", "CC", cc_half[1])
            if ck >= 2:
                wait("sync", "TE", te_after_ck[(h, ck - 2)])
            op("sync", "dma_start", out=ck_c[ck % 2][:, :].rearrange("p (c f) -> p c f", c=NDC),
               in_=kvch_out[h2].ap()[r * 2 + 0, :, :, t0h:t0h + 512], inc="LD", amt=16)
            op("sync", "dma_start", out=ck_s[ck % 2][:, :].rearrange("p (c f) -> p c f", c=NDC),
               in_=kvch_out[h2].ap()[r * 2 + 1, :, :, t0h:t0h + 512], inc="LD", amt=16)
            op("sync", "dma_start", out=ck_v[ck % 2][:, :].rearrange("p (c f) -> p c f", c=4),
               in_=kvv_out.ap()[r * NT + t0 // 128: r * NT + t0 // 128 + 4], inc="LD", amt=16)
            ld_ck = C["LD"]
            for qb in range(4):
                for kt in range(KT_PER):
                    ps = ps_sim[unit % 2]
                    if qb == 0 and kt == 0:
                        wait("tensor", "LD", ld_ck)
                        if ck == 0:
                            wait("tensor", "LD", ld_q)
                    if unit >= 2:
                        wait("tensor", "SE", exp_snap[unit - 2])
                    for cs in range(2):
                        kk = ck_c[ck % 2] if cs == 0 else ck_s[ck % 2]
                        qq = chq_c if cs == 0 else chq_s
                        for dc in range(NDC):
                            op("tensor", "matmul", ps[:, 0:256],
                               kk[:, dc * 512 + kt * 128: dc * 512 + (kt + 1) * 128],
                               qq[:, dc * 1024 + qb * 256: dc * 1024 + (qb + 1) * 256],
                               start=(cs == 0 and dc == 0), stop=(cs == 1 and dc == NDC - 1),
                               inc="TE" if (cs == 1 and dc == NDC - 1) else None)
                    sim_snap[unit] = C["TE"]
                    if pending_pv is not None:
                        emit_pv(*pending_pv)
                        pending_pv = None
                    wait("scalar", "TE", sim_snap[unit])
                    if unit >= 2:
                        wait("scalar", "TE", pv_snap.get(unit - 2, 0))
                    op("scalar", "activation", out=p_bf[unit % 2][:, :], in_=ps[:, 0:256], func=AF.Exp,
                       scale=1.0 / 32.0, bias=m32_col[:, 0:1], inc="SE")
                    exp_snap[unit] = C["SE"]
                    pending_pv = (unit, qb, kt, ck, kt == 0, kt == KT_PER - 1,
                                  ve_evac_q[qb] if kt == 0 else None)
                    unit += 1
                emit_pv(*pending_pv)
                pending_pv = None
                wait("vector", "TE", pv_snap[unit - 1])
                for sub in range(2):
                    qs = qb * 2 + sub
                    op("vector", "tensor_tensor", out=accum[:, qs * D:(qs + 1) * D],
                       in0=ps_out[sub][:, :], in1=accum[:, qs * D:(qs + 1) * D], op=OP.add, inc="VE")
                op("vector", "tensor_tensor", out=den_acc[:, qb * 2:qb * 2 + 2],
                   in0=ps_den[:, 0:2], in1=den_acc[:, qb * 2:qb * 2 + 2], op=OP.add, inc="VE")
                ve_evac_q[qb] = C["VE"]
            te_after_ck[(h, ck)] = C["TE"]
        drain()
        op("vector", "reciprocal", den_rc[:, :], den_acc[:, :], inc="VE")
        drain()
        for qs in range(8):
            key_ld = (h, qs - 2) if qs >= 2 else (h - 1, qs + 6)
            if key_ld in ve_x2_map:
                wait("sync", "VE", ve_x2_map[key_ld])
            op("sync", "dma_start", out=x1t[qs % 2][:, :], in_=x1sp.ap()[h * 8 + qs], inc="LD", amt=16)
            ld_x1 = C["LD"]
            wait("vector", "LD", ld_x1)
            key = (h, qs - 2) if qs >= 2 else (h - 1, qs + 6)
            if key in gp_x2:
                wait("vector", "GP", gp_x2[key])
            op("vector", "scalar_tensor_tensor", out=x2t[qs % 2][:, :], in0=accum[:, qs * D:(qs + 1) * D],
               scalar=den_rc[:, qs:qs + 1], in1=x1t[qs % 2][:, :], op0=OP.mult, op1=OP.add, inc="VE")
            ve_x2 = C["VE"]
            ve_x2_map[(h, qs)] = ve_x2
            wait("gpsimd", "VE", ve_x2)
            op("gpsimd", "dma_start", out=x2sp.ap()[h * 8 + qs], in_=x2t[qs % 2][:, :], inc="GP", amt=16)
            gp_x2[(h, qs)] = C["GP"]

    psD.close()
    scopeC.close()
    barrier()

    scopeE1 = contextlib.ExitStack()
    ctx["uT"] = scopeE1.enter_context(nc.sbuf_tensor("uT2", [128, NDC * SC], BF16))
    scopeE2 = contextlib.ExitStack()
    ctx["xbuf"] = scopeE2.enter_context(nc.sbuf_tensor("xbuf2", [128, NT * D], F32))
    ctx["wr4"] = [scopeE2.enter_context(nc.sbuf_tensor(f"wr4b_{i}", [128, NDC * D], BF16)) for i in range(3)]
    ctx["u_bf"] = [scopeE2.enter_context(nc.sbuf_tensor(f"u_bf2_{i}", [128, D], BF16)) for i in range(2)]
    psE = contextlib.ExitStack()
    ctx["ps_pair"] = [psE.enter_context(nc.psum_tensor(f"ps_pairb{i}", [128, D], F32)) for i in range(2)]
    psTR4 = contextlib.ExitStack()
    ctx["ps_tr"] = [psTR4.enter_context(nc.psum_tensor(f"ps_trb{i}", [128, D], BF16)) for i in range(2)]
    ctx["ps_sc"] = psTR4.enter_context(nc.psum_tensor("ps_scb", [128, 512], F32))
    psCD4 = contextlib.ExitStack()

    ld_x2 = []
    for t in range(NT):
        op("sync", "dma_start", out=ctx["xbuf"][:, t * D:(t + 1) * D], in_=x2sp.ap()[t], inc="LD", amt=16)
        ld_x2.append(C["LD"])
    wait("scalar", "LD", ld_x2[-1])
    wait("vector", "LD", ld_x2[-1])
    emit_ln(add_half_u=True)
    ln4_ve = C["VE"]
    emit_ffn(ln4_ve, [0, 0], [0, 0], out_dma=True,
             ps_trsc_stack=psTR4, ps_cd_stack=psCD4)

    wait("sync", "LD", C["LD"])
    wait("gpsimd", "GP", C["GP"])

    psCD4.close()
    psE.close()
    scopeE2.close()
    scopeE1.close()

    with nc.Block() as block:
        block.sync(lambda e: [f() for f in Q["sync"]])
        block.gpsimd(lambda e: [f() for f in Q["gpsimd"]])
        block.tensor(lambda e: [f() for f in Q["tensor"]])
        block.vector(lambda e: [f() for f in Q["vector"]])
        block.scalar(lambda e: [f() for f in Q["scalar"]])
    est.close()
    return nc


def kernel(L0_attn_x, L0_attn_Wq, L0_attn_bq, L0_attn_Wk, L0_attn_bk, L0_attn_Wv, L0_attn_bv,
           L0_attn_phase_bias, L0_ffn_centroids, L0_ffn_expert_biases, L0_ffn_Wr, L0_ffn_br,
           L0_ffn_Ws, L0_ffn_bs):
    x = np.ascontiguousarray(np.asarray(L0_attn_x, dtype=np.float32).reshape(S, D))
    wr = np.asarray(L0_ffn_Wr, dtype=np.float32)
    ws = np.asarray(L0_ffn_Ws, dtype=np.float32)
    thirds = [ws[0], ws[1], np.asarray(L0_attn_Wq, dtype=np.float32),
              np.asarray(L0_attn_Wk, dtype=np.float32), np.asarray(L0_attn_Wv, dtype=np.float32),
              np.zeros((D, D), np.float32), np.zeros((D, D), np.float32), np.zeros((D, D), np.float32)]
    ident = np.eye(128, dtype=ml_dtypes.bfloat16)
    common = {
        "bq": np.ascontiguousarray(L0_attn_bq, dtype=np.float32),
        "bk": np.ascontiguousarray(L0_attn_bk, dtype=np.float32),
        "bv": np.ascontiguousarray(L0_attn_bv, dtype=np.float32),
        "phb": np.ascontiguousarray(L0_attn_phase_bias, dtype=np.float32),
        "cen": np.ascontiguousarray(L0_ffn_centroids, dtype=np.float32),
        "ebi": np.ascontiguousarray(L0_ffn_expert_biases, dtype=np.float32),
        "brr": np.ascontiguousarray(L0_ffn_br, dtype=np.float32),
        "bs": np.ascontiguousarray(L0_ffn_bs, dtype=np.float32),
        "ident": np.asarray(ident),
    }
    in_maps = []
    for c in range(N_CORES):
        m = dict(common)
        m["x"] = np.ascontiguousarray(x[c * SC:(c + 1) * SC])
        m["wsrc"] = np.ascontiguousarray(np.stack([wr[2 * c], wr[2 * c + 1], thirds[c]], axis=0))
        in_maps.append(m)
    nc = build_nc()
    res = run_bass_kernel_spmd(nc, in_maps, core_ids=list(range(N_CORES)))
    outp = np.concatenate([np.asarray(res.results[c]["out"]) for c in range(N_CORES)], axis=0)
    return outp.reshape(1, S, D).astype(np.float32)


if __name__ == "__main__":
    import reference as _r
    ins = _r.setup_inputs()
    got = kernel(**{k: np.asarray(v) for k, v in ins.items()})
    exp = np.asarray(_r.reference(**ins))
    err = float(np.mean((got - exp) ** 2) / np.mean(exp ** 2))
    print("residual variance ratio:", err)


# version 18
# speedup vs baseline: 1.0182x; 1.0182x over previous
import math
import contextlib
import numpy as np
import ml_dtypes

import concourse.bass as bass
import concourse.mybir as mybir
from concourse.bass_utils import run_bass_kernel_spmd

F32 = mybir.dt.float32
BF16 = mybir.dt.bfloat16
AF = mybir.ActivationFunctionType
OP = mybir.AluOpType
AX = mybir.AxisListType

MAGIC = float(np.float32(1.5 * 2 ** 23))
TWO_PI = 2.0 * math.pi

N_CORES = 8
S = 16384
SC = S // N_CORES
D = 1024
NE = 16
NT = SC // 128
NDC = D // 128

ENGINES = ("sync", "gpsimd", "tensor", "vector", "scalar")

EXP_SLOT = [3 * (e // 2) + (e % 2) for e in range(NE)]
SLOT_WS0, SLOT_WS1, SLOT_WQ, SLOT_WK, SLOT_WV = 2, 5, 8, 11, 14


def build_nc():
    nc = bass.Bass("TRN2", target_bir_lowering=False)

    x_in = nc.dram_tensor("x", [SC, D], F32, kind="ExternalInput")
    wsrc = nc.dram_tensor("wsrc", [3, D, D], F32, kind="ExternalInput")
    bq = nc.dram_tensor("bq", [D], F32, kind="ExternalInput")
    bk = nc.dram_tensor("bk", [D], F32, kind="ExternalInput")
    bv = nc.dram_tensor("bv", [D], F32, kind="ExternalInput")
    phb = nc.dram_tensor("phb", [D], F32, kind="ExternalInput")
    cen = nc.dram_tensor("cen", [NE, D], F32, kind="ExternalInput")
    ebi = nc.dram_tensor("ebi", [NE], F32, kind="ExternalInput")
    brr = nc.dram_tensor("brr", [NE, D], F32, kind="ExternalInput")
    bs = nc.dram_tensor("bs", [2, D], F32, kind="ExternalInput")
    ident = nc.dram_tensor("ident", [128, 128], BF16, kind="ExternalInput")
    out = nc.dram_tensor("out", [SC, D], F32, kind="ExternalOutput")

    wtb_in = nc.dram_tensor("wtb_in", [3, NDC, 128, D], BF16)
    wtb_out = nc.dram_tensor("wtb_out", [24, NDC, 128, D], BF16, addr_space="Shared")
    kvch_in = [nc.dram_tensor(f"kvch_in{h}", [2, NDC, 128, SC // 2], BF16) for h in range(2)]
    kvch_out = [nc.dram_tensor(f"kvch_out{h}", [16, NDC, 128, SC // 2], BF16, addr_space="Shared")
                for h in range(2)]
    kvv_in = nc.dram_tensor("kvv_in", [NT, 128, D], BF16)
    kvv_out = nc.dram_tensor("kvv_out", [NT * 8, 128, D], BF16, addr_space="Shared")
    qsc = nc.dram_tensor("qsc", [2, NDC, 128, SC], BF16)
    x1sp = nc.dram_tensor("x1sp", [NT, 128, D], F32)
    x2sp = nc.dram_tensor("x2sp", [NT, 128, D], F32)

    xv = x_in.ap().rearrange("(t p) d -> t p d", p=128)
    outv = out.ap().rearrange("(t p) d -> t p d", p=128)

    Q = {e: [] for e in ENGINES}
    C = {"LD": 0, "GP": 0, "CC": 0, "TE": 0, "VE": 0, "SE": 0, "PH": 0}
    SEM = {}

    def op(eng, meth, *a, inc=None, amt=1, **kw):
        if inc is not None:
            C[inc] += amt

        def f(a=a, kw=kw, meth=meth, eng=eng, inc=inc, amt=amt):
            instr = getattr(getattr(nc, eng), meth)(*a, **kw)
            if inc is not None:
                instr.then_inc(SEM[inc], amt)
        Q[eng].append(f)
        return C[inc] if inc is not None else None

    def wait(eng, sem, val):
        if val is None or val <= 0:
            return
        Q[eng].append(lambda eng=eng, sem=sem, val=val: getattr(nc, eng).wait_ge(SEM[sem], val))

    def drain(eng="vector"):
        Q[eng].append(lambda eng=eng: getattr(nc, eng).drain())

    def collective(t_in, t_out):
        wait("gpsimd", "CC", C["CC"])
        op("gpsimd", "collective_compute", "AllGather", OP.bypass,
           replica_groups=[list(range(N_CORES))],
           ins=[t_in.ap().opt()], outs=[t_out.ap().opt()], inc="CC", amt=1)
        return C["CC"]

    def barrier():
        wait("sync", "LD", C["LD"])
        wait("gpsimd", "GP", C["GP"])
        C["PH"] += len(ENGINES)
        tgt = C["PH"]
        for e in ENGINES:
            Q[e].append(lambda e=e: getattr(nc, e).nop().then_inc(SEM["PH"], 1))
            wait(e, "PH", tgt)

    est = contextlib.ExitStack()
    for s in C:
        SEM[s] = est.enter_context(nc.semaphore(s))

    def sb(name, shape, dt):
        return est.enter_context(nc.sbuf_tensor(name, shape, dt))

    ident_sb = sb("ident_sb", [128, 128], BF16)
    ones_col = sb("ones_col", [128, 1], BF16)
    ones_row = sb("ones_row", [1, 128], F32)
    eps_col = sb("eps_col", [128, 1], F32)
    m32_col = sb("m32_col", [128, 1], F32)
    invwl = sb("invwl", [128, NDC], F32)
    scaleA = sb("scaleA", [128, NDC], F32)
    biasAk = sb("biasAk", [128, NDC], F32)
    biasAkC = sb("biasAkC", [128, NDC], F32)
    biasAq = sb("biasAq", [128, NDC], F32)
    biasAqC = sb("biasAqC", [128, NDC], F32)
    smallA = sb("smallA", [128, NDC], F32)
    smallB = sb("smallB", [128, NDC], F32)
    ct_sb = sb("ct_sb", [128, NDC * NE], BF16)
    br_bf = sb("br_bf", [NE, D], BF16)
    bs_row = sb("bs_row", [1, D], BF16)
    ones_rbf = sb("ones_rbf", [1, 128], BF16)
    eb_bc = sb("eb_bc", [128, NE], F32)
    bv_bc = sb("bv_bc", [128, D], BF16)
    wssum = sb("wssum", [128, NDC * D], BF16)
    st_sum = sb("st_sum", [128, NT], F32)
    st_ssq = sb("st_ssq", [128, NT], F32)
    st_mean = sb("st_mean", [128, NT], F32)
    st_ex2 = sb("st_ex2", [128, NT], F32)
    st_msq = sb("st_msq", [128, NT], F32)
    st_var = sb("st_var", [128, NT], F32)
    st_sd = sb("st_sd", [128, NT], F32)
    st_rstd = sb("st_rstd", [128, NT], F32)
    st_nmb = sb("st_nmb", [128, NT], F32)
    sq_scr = sb("sq_scr", [128, D], F32)
    g_s0 = sb("g_s0", [128, NT * NE], F32)
    g_exp = sb("g_exp", [128, NT * NE], F32)
    g_m1 = sb("g_m1", [128, NT * NE], F32)
    g_iv = sb("g_iv", [128, NT * NE], F32)
    g_t2 = g_s0
    g_m2 = g_iv
    g_ms = g_s0
    g_g = sb("g_g", [128, NT * NE], F32)
    g_bf = sb("g_bf", [128, NT * NE], BF16)
    gx_mx = sb("gx_mx", [128, NT], F32)
    gx_se = sb("gx_se", [128, NT], F32)
    gx_se2 = sb("gx_se2", [128, NT], F32)
    gx_rc = sb("gx_rc", [128, NT], F32)
    gx_m2 = sb("gx_m2", [128, NT], F32)
    gT = sb("gT", [NE, SC], BF16)

    def seg(t):
        return t[:, :].rearrange("p (t e) -> p t e", e=NE)

    def bcol(t):
        return t[:, :].unsqueeze(2).broadcast_to([128, NT, NE])

    ctx = {}

    scopeA1 = contextlib.ExitStack()
    ctx["uT"] = scopeA1.enter_context(nc.sbuf_tensor("uT", [128, NDC * SC], BF16))
    scopeA2 = contextlib.ExitStack()
    ctx["xbuf"] = scopeA2.enter_context(nc.sbuf_tensor("xbuf", [128, NT * D], F32))
    ctx["wr4"] = [scopeA2.enter_context(nc.sbuf_tensor(f"wr4_{i}", [128, NDC * D], BF16)) for i in range(3)]
    ctx["u_bf"] = [scopeA2.enter_context(nc.sbuf_tensor(f"u_bf_{i}", [128, D], BF16)) for i in range(2)]
    wbld = [scopeA2.enter_context(nc.sbuf_tensor(f"wbld_{i}", [128, D], BF16)) for i in range(2)]
    bs1_row = scopeA2.enter_context(nc.sbuf_tensor("bs1_row", [1, D], F32))
    eb_row = scopeA2.enter_context(nc.sbuf_tensor("eb_row", [1, NE], F32))
    bv_row = scopeA2.enter_context(nc.sbuf_tensor("bv_row", [1, D], F32))

    psA = contextlib.ExitStack()
    ctx["ps_pair"] = [psA.enter_context(nc.psum_tensor(f"ps_pair{i}", [128, D], F32)) for i in range(2)]
    psTR = contextlib.ExitStack()
    ctx["ps_tr"] = [psTR.enter_context(nc.psum_tensor(f"ps_tr{i}", [128, D], BF16)) for i in range(2)]
    ctx["ps_sc"] = psTR.enter_context(nc.psum_tensor("ps_sc", [128, 512], F32))
    psCD = contextlib.ExitStack()

    op("sync", "dma_start", out=ident_sb[:, :], in_=ident.ap(), inc="LD", amt=16)
    ld_ident = C["LD"]

    op("sync", "dma_start", out=eb_row[:, :], in_=ebi.ap().unsqueeze(0), inc="LD", amt=16)
    op("sync", "dma_start", out=smallA[:, :], in_=bk.ap().rearrange("(c p) -> p c", p=128), allow_slow_non_contiguous=True, inc="LD", amt=16)
    op("sync", "dma_start", out=smallB[:, :], in_=phb.ap().rearrange("(c p) -> p c", p=128), allow_slow_non_contiguous=True, inc="LD", amt=16)
    ld_small = C["LD"]
    op("sync", "dma_start", out=bv_row[:, :], in_=bs.ap()[0:1, :], inc="LD", amt=16)
    op("sync", "dma_start", out=bs1_row[:, :], in_=bs.ap()[1:2, :], inc="LD", amt=16)
    ld_bs = C["LD"]

    op("vector", "memset", ones_col[:, :], 1.0, inc="VE")
    op("vector", "memset", ones_row[:, :], 1.0, inc="VE")
    op("vector", "memset", ones_rbf[:, :], 1.0, inc="VE")
    op("vector", "memset", eps_col[:, :], 1e-5, inc="VE")
    op("vector", "memset", m32_col[:, :], -32.0, inc="VE")
    ve_ones = C["VE"]

    op("gpsimd", "iota", invwl[:, :], pattern=[[128, NDC]], base=1, channel_multiplier=1,
       allow_small_or_imprecise_dtypes=True, inc="GP", amt=1)
    gp_iota = C["GP"]
    wait("vector", "GP", gp_iota)
    op("vector", "tensor_scalar", out=invwl[:, :], in0=invwl[:, :], scalar1=TWO_PI / D, scalar2=1e-8,
       op0=OP.mult, op1=OP.add, inc="VE")
    drain()
    op("vector", "reciprocal", invwl[:, :], invwl[:, :], inc="VE")
    drain()
    op("vector", "tensor_scalar", out=scaleA[:, :], in0=invwl[:, :], scalar1=1.0 / TWO_PI, scalar2=None,
       op0=OP.mult, inc="VE")

    def emit_biasA(bsrc_tile, dst, dstC):
        op("vector", "tensor_tensor", out=dst[:, :], in0=bsrc_tile[:, :], in1=invwl[:, :], op=OP.mult, inc="VE")
        drain()
        op("vector", "tensor_tensor", out=dst[:, :], in0=dst[:, :], in1=smallB[:, :], op=OP.add, inc="VE")
        drain()
        op("vector", "tensor_scalar", out=dst[:, :], in0=dst[:, :], scalar1=1.0 / TWO_PI, scalar2=None,
           op0=OP.mult, inc="VE")
        drain()
        op("vector", "tensor_scalar", out=dstC[:, :], in0=dst[:, :], scalar1=0.25, scalar2=None,
           op0=OP.add, inc="VE")
        return C["VE"]

    wait("vector", "LD", ld_small)
    ve_biask = emit_biasA(smallA, biasAk, biasAkC)
    wait("sync", "VE", ve_biask)
    op("sync", "dma_start", out=smallA[:, :], in_=bq.ap().rearrange("(c p) -> p c", p=128), allow_slow_non_contiguous=True, inc="LD", amt=16)
    ld_bq = C["LD"]
    wait("vector", "LD", ld_bq)
    emit_biasA(smallA, biasAq, biasAqC)
    wait("vector", "LD", ld_bs)
    op("vector", "tensor_tensor", out=bs_row[:, :], in0=bv_row[:, :], in1=bs1_row[:, :], op=OP.add, inc="VE")
    ve_bsrow = C["VE"]


    ld_x = []
    for t in range(NT):
        op("sync", "dma_start", out=ctx["xbuf"][:, t * D:(t + 1) * D], in_=xv[t], inc="LD", amt=16)
        ld_x.append(C["LD"])

    te_slot_done = {}
    ve_bld = {}
    gp_bld = {}
    wi = 0
    for j in range(3):
        wb = ctx["wr4"][j]
        op("gpsimd", "dma_start", out=wb[:, :].rearrange("p (c d) -> p c d", c=NDC),
           in_=wsrc.ap()[j].rearrange("(c p) d -> p c d", p=128), inc="GP", amt=16)
        gp_cast = C["GP"]
        wait("tensor", "GP", gp_cast)
        if j == 0:
            wait("tensor", "LD", ld_ident)
        dst, jj = wtb_in, j
        for dc in range(NDC):
            pt = ctx["ps_tr"][wi % 2]
            bld = wbld[wi % 2]
            if wi >= 2:
                wait("tensor", "VE", ve_bld[wi - 2])
            for c in range(NDC):
                op("tensor", "transpose", pt[:, c * 128:(c + 1) * 128],
                   wb[:, c * D + dc * 128: c * D + (dc + 1) * 128], ident_sb[:, :],
                   inc="TE" if c == NDC - 1 else None)
            te_tp = C["TE"]
            wait("vector", "TE", te_tp)
            if wi >= 2:
                wait("vector", "GP", gp_bld[wi - 2])
            op("vector", "tensor_copy", out=bld[:, :], in_=pt[:, :], inc="VE")
            ve_bld[wi] = C["VE"]
            wait("gpsimd", "VE", ve_bld[wi])
            op("gpsimd", "dma_start", out=dst.ap()[jj, dc], in_=bld[:, :], inc="GP", amt=16)
            gp_bld[wi] = C["GP"]
            wi += 1
        te_slot_done[j] = C["TE"]
    ve_tr_all = C["VE"]
    wait("gpsimd", "GP", C["GP"])
    cc_wtb = collective(wtb_in, wtb_out)

    op("gpsimd", "dma_start", out=br_bf[:, :], in_=cen.ap(), inc="GP", amt=16)
    gp_cen = C["GP"]
    wait("tensor", "GP", gp_cen)
    wait("tensor", "VE", ve_tr_all)
    for dc in range(NDC):
        op("tensor", "transpose", ctx["ps_tr"][0][:, dc * NE:(dc + 1) * NE],
           br_bf[:, dc * 128:(dc + 1) * 128], ident_sb[0:NE, 0:NE],
           inc="TE" if dc == NDC - 1 else None)
    te_ct = C["TE"]
    wait("vector", "TE", te_ct)
    op("vector", "tensor_copy", out=ct_sb[:, :], in_=ctx["ps_tr"][0][:, 0:NDC * NE], inc="VE")
    ve_ct = C["VE"]

    wait("gpsimd", "TE", te_ct)
    op("gpsimd", "dma_start", out=br_bf[:, :], in_=brr.ap(), inc="GP", amt=16)
    gp_br = C["GP"]

    wait("tensor", "LD", ld_bs)
    wait("tensor", "VE", ve_ones)
    op("tensor", "matmul", ctx["ps_pair"][0][:, 0:NE], ones_row[:, :], eb_row[:, :],
       start=True, stop=True, inc="TE")
    te_eb = C["TE"]
    wait("vector", "TE", te_eb)
    op("vector", "tensor_copy", out=eb_bc[:, :], in_=ctx["ps_pair"][0][:, 0:NE], inc="VE")
    ve_eb = C["VE"]
    wait("sync", "VE", ve_bsrow)
    op("sync", "dma_start", out=bs1_row[:, :], in_=bv.ap().unsqueeze(0), inc="LD", amt=16)
    ld_bv2 = C["LD"]
    wait("tensor", "LD", ld_bv2)
    op("tensor", "matmul", ctx["ps_pair"][1][:, 0:512], ones_row[:, :], bs1_row[:, 0:512], start=True, stop=True)
    op("tensor", "matmul", ctx["ps_pair"][1][:, 512:1024], ones_row[:, :], bs1_row[:, 512:1024],
       start=True, stop=True, inc="TE")
    te_bv = C["TE"]
    wait("vector", "TE", te_bv)
    op("vector", "tensor_copy", out=bv_bc[:, :], in_=ctx["ps_pair"][1][:, :], inc="VE")
    ve_bv = C["VE"]

    wait("sync", "CC", cc_wtb)
    wait("sync", "TE", te_slot_done[2])
    op("sync", "dma_start", out=ctx["wr4"][0][:, :].rearrange("p (c d) -> p c d", c=NDC),
       in_=wtb_out.ap()[SLOT_WS0].rearrange("c p d -> p c d"), inc="LD", amt=16)
    op("sync", "dma_start", out=ctx["wr4"][1][:, :].rearrange("p (c d) -> p c d", c=NDC),
       in_=wtb_out.ap()[SLOT_WS1].rearrange("c p d -> p c d"), inc="LD", amt=16)
    ld_ws = C["LD"]
    wait("vector", "LD", ld_ws)
    op("vector", "tensor_tensor", out=wssum[:, :], in0=ctx["wr4"][0][:, :], in1=ctx["wr4"][1][:, :],
       op=OP.add, inc="VE")
    ve_wss = C["VE"]

    def emit_ln(add_half_u, spill_to=None, stats_ready=False):
        xb = ctx["xbuf"]
        uT = ctx["uT"]
        if not stats_ready:
            for t in range(NT):
                xt = xb[:, t * D:(t + 1) * D]
                op("scalar", "activation", out=sq_scr[:, :], in_=xt, func=AF.Square,
                   accum_out=st_ssq[:, t:t + 1], inc="SE")
                op("vector", "tensor_reduce", out=st_sum[:, t:t + 1], in_=xt, axis=AX.X, op=OP.add, inc="VE")
        se_sq = C["SE"]
        drain()
        wait("vector", "SE", se_sq)
        op("vector", "tensor_scalar", out=st_mean[:, :], in0=st_sum[:, :], scalar1=1.0 / D, scalar2=None,
           op0=OP.mult, inc="VE")
        op("vector", "tensor_scalar", out=st_ex2[:, :], in0=st_ssq[:, :], scalar1=1.0 / D, scalar2=None,
           op0=OP.mult, inc="VE")
        op("vector", "tensor_tensor", out=st_msq[:, :], in0=st_mean[:, :], in1=st_mean[:, :], op=OP.mult, inc="VE")
        drain()
        op("vector", "tensor_tensor", out=st_var[:, :], in0=st_ex2[:, :], in1=st_msq[:, :], op=OP.subtract, inc="VE")
        ve_var = C["VE"]
        wait("scalar", "VE", ve_var)
        op("scalar", "activation", out=st_sd[:, :], in_=st_var[:, :], func=AF.Sqrt, bias=eps_col[:, 0:1], inc="SE")
        se_sd = C["SE"]
        wait("vector", "SE", se_sd)
        op("vector", "reciprocal", st_rstd[:, :], st_sd[:, :], inc="VE")
        drain()
        op("vector", "scalar_tensor_tensor", out=st_nmb[:, :], in0=st_mean[:, :], scalar=-1.0,
           in1=st_rstd[:, :], op0=OP.mult, op1=OP.mult, inc="VE")
        drain()
        ve_stats = C["VE"]

        ve_after = {}
        te_after = {}
        ve_u = {}
        for t in range(NT):
            xt = xb[:, t * D:(t + 1) * D]
            ub = ctx["u_bf"][t % 2]
            pt = ctx["ps_tr"][t % 2]
            if t >= 2:
                wait("vector", "TE", te_after[t - 2])
            op("vector", "tensor_scalar", out=ub[:, :], in0=xt, scalar1=st_rstd[:, t:t + 1],
               scalar2=st_nmb[:, t:t + 1], op0=OP.mult, op1=OP.add, inc="VE")
            ve_u[t] = C["VE"]
            if add_half_u:
                drain()
                op("vector", "scalar_tensor_tensor", out=xt, in0=ub[:, :], scalar=0.5, in1=xt,
                   op0=OP.mult, op1=OP.add, inc="VE")
            wait("tensor", "VE", ve_u[t])
            if t >= 2:
                wait("tensor", "VE", ve_after[t - 2])
            for dc in range(NDC):
                op("tensor", "transpose", pt[:, dc * 128:(dc + 1) * 128],
                   ub[:, dc * 128:(dc + 1) * 128], ident_sb[:, :],
                   inc="TE" if dc == NDC - 1 else None)
            te_after[t] = C["TE"]
            wait("vector", "TE", te_after[t])
            op("vector", "tensor_copy",
               out=uT[:, :].rearrange("p (c s) -> p c s", c=NDC)[:, :, t * 128:(t + 1) * 128],
               in_=pt[:, :].rearrange("p (c s) -> p c s", c=NDC), inc="VE")
            ve_after[t] = C["VE"]
            if spill_to is not None:
                wait("gpsimd", "VE", ve_u[t])
                op("gpsimd", "dma_start", out=spill_to.ap()[t], in_=xt, inc="GP", amt=16)
        return ve_after[NT - 1]

    def emit_ffn(ln_ve, init_pair_free, wr_init_free, out_dma, ps_trsc_stack, ps_cd_stack):
        xb = ctx["xbuf"]
        uT = ctx["uT"]
        wr4 = ctx["wr4"]
        ps_pair = ctx["ps_pair"]
        ps_sc = ctx["ps_sc"]
        unit = 0
        ve_evac = {}

        def pair_wait(u):
            if u >= 2:
                wait("tensor", "VE", ve_evac[u - 2])
            elif init_pair_free[u]:
                wait("tensor", "VE", init_pair_free[u])

        wait("tensor", "VE", ln_ve)
        wait("tensor", "VE", ve_ct)
        for t in range(NT):
            for dc in range(NDC):
                lhs = uT[:, dc * SC + t * 128: dc * SC + (t + 1) * 128]
                op("tensor", "matmul", ps_sc[:, t * NE:(t + 1) * NE], lhs, ct_sb[:, dc * NE:(dc + 1) * NE],
                   start=(dc == 0), stop=(dc == NDC - 1),
                   inc="TE" if (dc == NDC - 1 and t == NT - 1) else None)
        te_scores = C["TE"]

        wait("vector", "TE", te_scores)
        op("vector", "tensor_tensor", out=seg(g_s0),
           in0=ps_sc[:, 0:NT * NE].rearrange("p (t e) -> p t e", e=NE),
           in1=eb_bc[:, :].unsqueeze(1).broadcast_to([128, NT, NE]), op=OP.add, inc="VE")
        drain()
        op("vector", "tensor_reduce", out=gx_mx[:, :], in_=seg(g_s0), axis=AX.X, op=OP.max, inc="VE")
        drain()
        op("vector", "tensor_tensor", out=seg(g_s0), in0=seg(g_s0), in1=bcol(gx_mx), op=OP.subtract, inc="VE")
        ve_g1 = C["VE"]
        wait("scalar", "VE", ve_g1)
        op("scalar", "activation", out=g_exp[:, :], in_=g_s0[:, :], func=AF.Exp, inc="SE")
        se_gexp = C["SE"]
        wait("vector", "SE", se_gexp)
        op("vector", "tensor_reduce", out=gx_se[:, :], in_=seg(g_exp), axis=AX.X, op=OP.add, inc="VE")
        drain()
        op("vector", "tensor_scalar", out=gx_se2[:, :], in0=gx_se[:, :], scalar1=2.0, scalar2=None,
           op0=OP.mult, inc="VE")
        op("vector", "tensor_scalar", out=g_m1[:, :], in0=g_exp[:, :], scalar1=1.0, scalar2=None,
           op0=OP.is_ge, inc="VE")
        op("vector", "reciprocal", gx_rc[:, :], gx_se2[:, :], inc="VE")
        op("vector", "tensor_scalar", out=g_iv[:, :], in0=g_m1[:, :], scalar1=-1.0, scalar2=1.0,
           op0=OP.mult, op1=OP.add, inc="VE")
        drain()
        op("vector", "tensor_tensor", out=g_t2[:, :], in0=g_exp[:, :], in1=g_iv[:, :], op=OP.mult, inc="VE")
        drain()
        op("vector", "tensor_reduce", out=gx_m2[:, :], in_=seg(g_t2), axis=AX.X, op=OP.max, inc="VE")
        drain()
        op("vector", "tensor_tensor", out=seg(g_m2), in0=seg(g_t2), in1=bcol(gx_m2), op=OP.is_ge, inc="VE")
        drain()
        op("vector", "tensor_tensor", out=g_ms[:, :], in0=g_m1[:, :], in1=g_m2[:, :], op=OP.add, inc="VE")
        drain()
        op("vector", "tensor_tensor", out=g_iv[:, :], in0=g_exp[:, :], in1=g_ms[:, :], op=OP.mult, inc="VE")
        drain()
        op("vector", "tensor_tensor", out=seg(g_g), in0=seg(g_iv), in1=bcol(gx_rc), op=OP.mult, inc="VE")
        drain()
        op("vector", "tensor_copy", out=g_bf[:, :], in_=g_g[:, :], inc="VE")
        ve_gbf = C["VE"]

        wait("tensor", "VE", ve_gbf)
        ve_gT = None
        ve_gt_cp = {}
        for t in range(NT):
            pt = ctx["ps_tr"][t % 2]
            if t >= 2:
                wait("tensor", "VE", ve_gt_cp[t - 2])
            op("tensor", "transpose", pt[0:NE, 0:128], g_bf[:, t * NE:(t + 1) * NE], ident_sb[:, :], inc="TE")
            te_g = C["TE"]
            wait("vector", "TE", te_g)
            op("vector", "tensor_copy", out=gT[:, t * 128:(t + 1) * 128], in_=pt[0:NE, 0:128], inc="VE")
            ve_gt_cp[t] = ve_gT = C["VE"]

        ps_trsc_stack.close()
        pairs4 = ps_pair + [ps_cd_stack.enter_context(
            nc.psum_tensor(f"ps_cd{i}_{1 if out_dma else 0}", [128, D], F32)) for i in range(2)]

        wait("tensor", "VE", ve_gT)
        wait("tensor", "GP", gp_br)
        for t in range(NT):
            pp = ps_pair[unit % 2]
            pair_wait(unit)
            lhs = gT[:, t * 128:(t + 1) * 128]
            op("tensor", "matmul", pp[:, 0:512], lhs, br_bf[:, 0:512], start=True, stop=True)
            op("tensor", "matmul", pp[:, 512:1024], lhs, br_bf[:, 512:1024], start=True, stop=True, inc="TE")
            te_u = C["TE"]
            wait("vector", "TE", te_u)
            op("vector", "tensor_tensor", out=xb[:, t * D:(t + 1) * D], in0=pp[:, :],
               in1=xb[:, t * D:(t + 1) * D], op=OP.add, inc="VE")
            ve_evac[unit] = C["VE"]
            unit += 1

        ld_done = {}

        def emit_load(e, free_ve):
            if free_ve:
                wait("sync", "VE", free_ve)
            op("sync", "dma_start", out=wr4[e % 2][:, :].rearrange("p (c d) -> p c d", c=NDC),
               in_=wtb_out.ap()[EXP_SLOT[e]].rearrange("c p d -> p c d"), inc="LD", amt=16)
            ld_done[e] = C["LD"]

        wait("sync", "CC", cc_wtb)
        emit_load(0, wr_init_free[0])
        emit_load(1, wr_init_free[1])
        ve_pair_evac = {}
        unit0 = unit
        for e in range(NE):
            wait("tensor", "LD", ld_done[e])
            wb = wr4[e % 2]
            for t in range(NT):
                pp = pairs4[unit % 4]
                if unit >= unit0 + 4:
                    wait("tensor", "VE", ve_evac[unit - 4])
                elif unit == unit0:
                    wait("tensor", "VE", ve_evac[unit0 - 2])
                elif unit == unit0 + 1:
                    wait("tensor", "VE", ve_evac[unit0 - 1])
                for dc in range(NDC):
                    lhs = uT[:, dc * SC + t * 128: dc * SC + (t + 1) * 128]
                    op("tensor", "matmul", pp[:, 0:512], lhs, wb[:, dc * D: dc * D + 512],
                       start=(dc == 0), stop=(dc == NDC - 1))
                    op("tensor", "matmul", pp[:, 512:1024], lhs, wb[:, dc * D + 512:(dc + 1) * D],
                       start=(dc == 0), stop=(dc == NDC - 1),
                       inc="TE" if dc == NDC - 1 else None)
                te_u = C["TE"]
                wait("vector", "TE", te_u)
                op("vector", "scalar_tensor_tensor", out=xb[:, t * D:(t + 1) * D], in0=pp[:, :],
                   scalar=g_g[:, t * NE + e: t * NE + e + 1], in1=xb[:, t * D:(t + 1) * D],
                   op0=OP.mult, op1=OP.add, inc="VE")
                ve_evac[unit] = C["VE"]
                unit += 1
            ve_pair_evac[(NE // 2 - 1, NT - 2)] = ve_evac[unit - 2]
            ve_pair_evac[(NE // 2 - 1, NT - 1)] = ve_evac[unit - 1]
            if e + 2 < NE:
                emit_load(e + 2, ve_evac[unit - 1])

        wait("tensor", "VE", ve_wss)
        unit2 = 0
        ve_sh = {}
        for t in range(NT):
            pp = ps_pair[unit2 % 2]
            key = (NE // 2 - 1, t) if unit2 < 2 else None
            if unit2 >= 2:
                wait("tensor", "VE", ve_sh[unit2 - 2])
            else:
                wait("tensor", "VE", ve_pair_evac[(NE // 2 - 1, NT - 2 + unit2)])
            op("tensor", "matmul", pp[:, 0:512], ones_rbf[:, :], bs_row[:, 0:512], start=True, stop=False)
            op("tensor", "matmul", pp[:, 512:1024], ones_rbf[:, :], bs_row[:, 512:1024], start=True, stop=False)
            for dc in range(NDC):
                lhs = uT[:, dc * SC + t * 128: dc * SC + (t + 1) * 128]
                op("tensor", "matmul", pp[:, 0:512], lhs, wssum[:, dc * D: dc * D + 512],
                   start=False, stop=(dc == NDC - 1))
                op("tensor", "matmul", pp[:, 512:1024], lhs, wssum[:, dc * D + 512: (dc + 1) * D],
                   start=False, stop=(dc == NDC - 1),
                   inc="TE" if dc == NDC - 1 else None)
            te_u = C["TE"]
            wait("vector", "TE", te_u)
            op("vector", "scalar_tensor_tensor", out=xb[:, t * D:(t + 1) * D], in0=pp[:, :],
               scalar=0.25, in1=xb[:, t * D:(t + 1) * D], op0=OP.mult, op1=OP.add, inc="VE")
            ve_sh[unit2] = C["VE"]
            if out_dma:
                wait("sync", "VE", ve_sh[unit2])
                op("sync", "dma_start", out=outv[t], in_=xb[:, t * D:(t + 1) * D], inc="LD", amt=16)
            unit2 += 1
        return ve_sh[unit2 - 1]

    wait("scalar", "LD", ld_x[-1])
    wait("vector", "LD", ld_x[-1])
    emit_ln(add_half_u=True)
    ln1_ve = C["VE"]
    ffn1_ve = emit_ffn(ln1_ve, [ve_eb, ve_bv], [ve_wss, 0], out_dma=False,
                       ps_trsc_stack=psTR, ps_cd_stack=psCD)

    psCD.close()
    psTR2 = contextlib.ExitStack()
    ctx["ps_tr"] = [psTR2.enter_context(nc.psum_tensor(f"ps_tr2_{i}", [128, D], BF16)) for i in range(2)]
    wait("scalar", "VE", ffn1_ve)
    wait("gpsimd", "VE", ffn1_ve)
    emit_ln(add_half_u=False, spill_to=x1sp)
    ln2_ve = C["VE"]

    psTR2.close()
    psA.close()
    scopeA2.close()
    barrier()

    scopeB = contextlib.ExitStack()
    wqkv = scopeB.enter_context(nc.sbuf_tensor("wqkv", [128, NDC * D], BF16))
    csk_c = [scopeB.enter_context(nc.sbuf_tensor(f"csk_c{i}", [128, SC], BF16)) for i in range(2)]
    csk_s = [scopeB.enter_context(nc.sbuf_tensor(f"csk_s{i}", [128, SC], BF16)) for i in range(2)]
    tf_s = [scopeB.enter_context(nc.sbuf_tensor(f"tf_s{i}", [128, 512], F32)) for i in range(2)]
    tf_c = [scopeB.enter_context(nc.sbuf_tensor(f"tf_c{i}", [128, 512], F32)) for i in range(2)]
    tk_s = scopeB.enter_context(nc.sbuf_tensor("tk_s", [128, 512], F32))
    tk_c = scopeB.enter_context(nc.sbuf_tensor("tk_c", [128, 512], F32))
    v_bf = [scopeB.enter_context(nc.sbuf_tensor(f"v_bf{i}", [128, D], BF16)) for i in range(2)]
    psC = contextlib.ExitStack()
    ps_k = [psC.enter_context(nc.psum_tensor(f"ps_k{i}", [128, 512], F32)) for i in range(4)]
    ps_v = [psC.enter_context(nc.psum_tensor(f"ps_v{i}", [128, D], F32)) for i in range(2)]

    def emit_trig_pass(slot, bA, bAC, dst, half_done=None):
        uT = ctx["uT"]
        op("sync", "dma_start", out=wqkv[:, :].rearrange("p (c d) -> p c d", c=NDC),
           in_=wtb_out.ap()[slot].rearrange("c p d -> p c d"), inc="LD", amt=16)
        ld_w = C["LD"]
        wait("tensor", "LD", ld_w)
        wait("tensor", "VE", C["VE"])
        se_psum = {}
        ve_f = {}
        se_sin = {}
        gp_out = {}
        u2 = 0
        for sbk in range(4):
            for ec in range(NDC):
                pk = ps_k[u2 % 4]
                st_c = csk_c[0][:, (u2 % 4) * 512:(u2 % 4 + 1) * 512]
                st_s = csk_s[0][:, (u2 % 4) * 512:(u2 % 4 + 1) * 512]
                if u2 >= 4:
                    wait("tensor", "VE", se_psum[u2 - 4])
                for dc in range(NDC):
                    op("tensor", "matmul", pk[:, :],
                       wqkv[:, dc * D + ec * 128: dc * D + (ec + 1) * 128],
                       uT[:, dc * SC + sbk * 512: dc * SC + (sbk + 1) * 512],
                       start=(dc == 0), stop=(dc == NDC - 1),
                       inc="TE" if dc == NDC - 1 else None)
                te_k = C["TE"]
                wait("vector", "TE", te_k)
                if u2 >= 2:
                    wait("vector", "SE", se_sin[u2 - 2])
                op("vector", "tensor_scalar", out=tf_s[u2 % 2][:, :], in0=pk[:, :],
                   scalar1=scaleA[:, ec:ec + 1], scalar2=bA[:, ec:ec + 1],
                   op0=OP.mult, op1=OP.add, inc="VE")
                op("vector", "tensor_scalar", out=tf_c[u2 % 2][:, :], in0=pk[:, :],
                   scalar1=scaleA[:, ec:ec + 1], scalar2=bAC[:, ec:ec + 1],
                   op0=OP.mult, op1=OP.add, inc="VE")
                se_psum[u2] = C["VE"]
                op("vector", "tensor_scalar", out=tk_s[:, :], in0=tf_s[u2 % 2][:, :], scalar1=MAGIC,
                   scalar2=MAGIC, op0=OP.add, op1=OP.subtract, inc="VE")
                op("vector", "tensor_scalar", out=tk_c[:, :], in0=tf_c[u2 % 2][:, :], scalar1=MAGIC,
                   scalar2=MAGIC, op0=OP.add, op1=OP.subtract, inc="VE")
                op("vector", "tensor_tensor", out=tf_s[u2 % 2][:, :], in0=tf_s[u2 % 2][:, :],
                   in1=tk_s[:, :], op=OP.subtract, inc="VE")
                op("vector", "tensor_tensor", out=tf_c[u2 % 2][:, :], in0=tf_c[u2 % 2][:, :],
                   in1=tk_c[:, :], op=OP.subtract, inc="VE")
                ve_f[u2] = C["VE"]
                wait("scalar", "VE", ve_f[u2])
                if u2 >= 4:
                    wait("scalar", "LD", gp_out[u2 - 4])
                op("scalar", "activation", out=st_s, in_=tf_s[u2 % 2][:, :],
                   func=AF.Sin, scale=TWO_PI, inc="SE")
                op("scalar", "activation", out=st_c, in_=tf_c[u2 % 2][:, :],
                   func=AF.Sin, scale=TWO_PI, inc="SE")
                se_sin[u2] = C["SE"]
                wait("sync", "SE", se_sin[u2])
                op("sync", "dma_start", out=dst(0, ec, sbk), in_=st_c, inc="LD", amt=16)
                op("sync", "dma_start", out=dst(1, ec, sbk), in_=st_s, inc="LD", amt=16)
                gp_out[u2] = C["LD"]
                u2 += 1
            if half_done is not None and sbk % 2 == 1:
                wait("gpsimd", "LD", C["LD"])
                half_done(sbk // 2)
        return C["TE"]

    op("sync", "dma_start", out=wqkv[:, :].rearrange("p (c d) -> p c d", c=NDC),
       in_=wtb_out.ap()[SLOT_WV].rearrange("c p d -> p c d"), inc="LD", amt=16)
    ld_wv = C["LD"]
    wait("tensor", "LD", ld_wv)
    wait("tensor", "VE", ln2_ve)
    ve_v = {}
    gp_v = {}
    for t in range(NT):
        pv = ps_v[t % 2]
        if t >= 2:
            wait("tensor", "VE", ve_v[t - 2])
        uT = ctx["uT"]
        for dc in range(NDC):
            lhs = uT[:, dc * SC + t * 128: dc * SC + (t + 1) * 128]
            op("tensor", "matmul", pv[:, 0:512], lhs, wqkv[:, dc * D: dc * D + 512],
               start=(dc == 0), stop=(dc == NDC - 1))
            op("tensor", "matmul", pv[:, 512:1024], lhs, wqkv[:, dc * D + 512:(dc + 1) * D],
               start=(dc == 0), stop=(dc == NDC - 1),
               inc="TE" if dc == NDC - 1 else None)
        te_v = C["TE"]
        wait("vector", "TE", te_v)
        if t >= 2:
            wait("vector", "LD", gp_v[t - 2])
        op("vector", "tensor_tensor", out=v_bf[t % 2][:, :], in0=pv[:, :], in1=bv_bc[:, :], op=OP.add, inc="VE")
        ve_v[t] = C["VE"]
        wait("sync", "VE", ve_v[t])
        op("sync", "dma_start", out=kvv_in.ap()[t], in_=v_bf[t % 2][:, :], inc="LD", amt=16)
        gp_v[t] = C["LD"]

    wait("gpsimd", "LD", C["LD"])
    cc_kvv = collective(kvv_in, kvv_out)

    cc_half = {}

    def k_half_done(h):
        cc_half[h] = collective(kvch_in[h], kvch_out[h])

    def k_dst(region, ec, sbk):
        return kvch_in[sbk // 2].ap()[region, ec, :, (sbk % 2) * 512:(sbk % 2 + 1) * 512]

    wait("sync", "TE", C["TE"])
    emit_trig_pass(SLOT_WK, biasAk, biasAkC, k_dst, half_done=k_half_done)

    wait("sync", "TE", C["TE"])
    wait("scalar", "GP", C["GP"])
    emit_trig_pass(SLOT_WQ, biasAq, biasAqC,
                   lambda region, ec, sbk: qsc.ap()[region, ec, :, sbk * 512:(sbk + 1) * 512])

    psC.close()
    scopeB.close()
    scopeA1.close()
    barrier()

    scopeC = contextlib.ExitStack()
    chq_c2 = [scopeC.enter_context(nc.sbuf_tensor(f"chq_c{i}", [128, NDC * 1024], BF16)) for i in range(2)]
    chq_s2 = [scopeC.enter_context(nc.sbuf_tensor(f"chq_s{i}", [128, NDC * 1024], BF16)) for i in range(2)]
    ck_c = [scopeC.enter_context(nc.sbuf_tensor(f"ck_c{i}", [128, NDC * 512], BF16)) for i in range(2)]
    ck_s = [scopeC.enter_context(nc.sbuf_tensor(f"ck_s{i}", [128, NDC * 512], BF16)) for i in range(2)]
    ck_v = [scopeC.enter_context(nc.sbuf_tensor(f"ck_v{i}", [128, 4 * D], BF16)) for i in range(2)]
    accum = scopeC.enter_context(nc.sbuf_tensor("accum", [128, 8 * D], F32))
    den_acc = scopeC.enter_context(nc.sbuf_tensor("den_acc", [128, 8], F32))
    den_rc = scopeC.enter_context(nc.sbuf_tensor("den_rc", [128, 8], F32))
    p_bf = [scopeC.enter_context(nc.sbuf_tensor(f"p_bf{i}", [128, 256], BF16)) for i in range(2)]
    x1t = [scopeC.enter_context(nc.sbuf_tensor(f"x1t{i}", [128, D], F32)) for i in range(2)]
    x2t = [scopeC.enter_context(nc.sbuf_tensor(f"x2t{i}", [128, D], F32)) for i in range(2)]
    psD = contextlib.ExitStack()
    ps_sim = [psD.enter_context(nc.psum_tensor(f"ps_sim{i}", [128, 512], F32)) for i in range(2)]
    ps_out = [psD.enter_context(nc.psum_tensor(f"ps_out{i}", [128, D], F32)) for i in range(2)]
    ps_den = psD.enter_context(nc.psum_tensor("ps_den", [128, 512], F32))

    NCK = 32
    KT_PER = 4

    sim_snap = {}
    exp_snap = {}
    pv_snap = {}
    unit = 0
    pending_pv = None
    te_after_ck = {}
    gp_x2 = {}
    ve_x2_map = {}

    def emit_pv(u, qb, kt, ck, first, last, ve_free):
        if first and ve_free:
            wait("tensor", "VE", ve_free)
        wait("tensor", "SE", exp_snap[u])
        pb = p_bf[u % 2]
        vv = ck_v[ck % 2]
        for sub in range(2):
            po = ps_out[sub]
            lhs = pb[:, sub * 128:(sub + 1) * 128]
            op("tensor", "matmul", po[:, 0:512], lhs, vv[:, kt * D: kt * D + 512],
               start=first, stop=last)
            op("tensor", "matmul", po[:, 512:1024], lhs, vv[:, kt * D + 512:(kt + 1) * D],
               start=first, stop=last)
            op("tensor", "matmul", ps_den[:, sub:sub + 1], lhs, ones_col[:, :],
               start=(first and sub == 0), stop=last, inc="TE" if sub == 1 else None)
        pv_snap[u] = C["TE"]

    for h in range(2):
        op("vector", "memset", accum[:, :], 0.0, inc="VE")
        op("vector", "memset", den_acc[:, :], 0.0, inc="VE")
        ve_zero = C["VE"]
        chq_c = chq_c2[h % 2]
        chq_s = chq_s2[h % 2]
        op("sync", "dma_start", out=chq_c[:, :].rearrange("p (c f) -> p c f", c=NDC),
           in_=qsc.ap()[0, :, :, h * 1024:(h + 1) * 1024], inc="LD", amt=16)
        op("sync", "dma_start", out=chq_s[:, :].rearrange("p (c f) -> p c f", c=NDC),
           in_=qsc.ap()[1, :, :, h * 1024:(h + 1) * 1024], inc="LD", amt=16)
        ld_q = C["LD"]
        ve_evac_q = {qb: ve_zero for qb in range(4)}
        ck_order = [(t0i, r) for t0i in range(4) for r in range(8)]
        for ck in range(NCK):
            t0i, r = ck_order[ck]
            t0 = t0i * 512
            h2, t0h = t0 // 1024, t0 % 1024
            if ck == 0:
                wait("sync", "CC", max(cc_kvv, cc_half[0]))
            if t0i == 2 and r == 0:
                wait("sync", "CC", cc_half[1])
            if ck >= 2:
                wait("sync", "TE", te_after_ck[(h, ck - 2)])
            op("sync", "dma_start", out=ck_c[ck % 2][:, :].rearrange("p (c f) -> p c f", c=NDC),
               in_=kvch_out[h2].ap()[r * 2 + 0, :, :, t0h:t0h + 512], inc="LD", amt=16)
            op("sync", "dma_start", out=ck_s[ck % 2][:, :].rearrange("p (c f) -> p c f", c=NDC),
               in_=kvch_out[h2].ap()[r * 2 + 1, :, :, t0h:t0h + 512], inc="LD", amt=16)
            op("sync", "dma_start", out=ck_v[ck % 2][:, :].rearrange("p (c f) -> p c f", c=4),
               in_=kvv_out.ap()[r * NT + t0 // 128: r * NT + t0 // 128 + 4], inc="LD", amt=16)
            ld_ck = C["LD"]
            for qb in range(4):
                for kt in range(KT_PER):
                    ps = ps_sim[unit % 2]
                    if qb == 0 and kt == 0:
                        wait("tensor", "LD", ld_ck)
                        if ck == 0:
                            wait("tensor", "LD", ld_q)
                    if unit >= 2:
                        wait("tensor", "SE", exp_snap[unit - 2])
                    for cs in range(2):
                        kk = ck_c[ck % 2] if cs == 0 else ck_s[ck % 2]
                        qq = chq_c if cs == 0 else chq_s
                        for dc in range(NDC):
                            op("tensor", "matmul", ps[:, 0:256],
                               kk[:, dc * 512 + kt * 128: dc * 512 + (kt + 1) * 128],
                               qq[:, dc * 1024 + qb * 256: dc * 1024 + (qb + 1) * 256],
                               start=(cs == 0 and dc == 0), stop=(cs == 1 and dc == NDC - 1),
                               inc="TE" if (cs == 1 and dc == NDC - 1) else None)
                    sim_snap[unit] = C["TE"]
                    if pending_pv is not None:
                        emit_pv(*pending_pv)
                        pending_pv = None
                    wait("scalar", "TE", sim_snap[unit])
                    if unit >= 2:
                        wait("scalar", "TE", pv_snap.get(unit - 2, 0))
                    op("scalar", "activation", out=p_bf[unit % 2][:, :], in_=ps[:, 0:256], func=AF.Exp,
                       scale=1.0 / 32.0, bias=m32_col[:, 0:1], inc="SE")
                    exp_snap[unit] = C["SE"]
                    pending_pv = (unit, qb, kt, ck, kt == 0, kt == KT_PER - 1,
                                  ve_evac_q[qb] if kt == 0 else None)
                    unit += 1
                emit_pv(*pending_pv)
                pending_pv = None
                wait("vector", "TE", pv_snap[unit - 1])
                for sub in range(2):
                    qs = qb * 2 + sub
                    op("vector", "tensor_tensor", out=accum[:, qs * D:(qs + 1) * D],
                       in0=ps_out[sub][:, :], in1=accum[:, qs * D:(qs + 1) * D], op=OP.add, inc="VE")
                op("vector", "tensor_tensor", out=den_acc[:, qb * 2:qb * 2 + 2],
                   in0=ps_den[:, 0:2], in1=den_acc[:, qb * 2:qb * 2 + 2], op=OP.add, inc="VE")
                ve_evac_q[qb] = C["VE"]
            te_after_ck[(h, ck)] = C["TE"]
        drain()
        op("vector", "reciprocal", den_rc[:, :], den_acc[:, :], inc="VE")
        drain()
        for qs in range(8):
            key_ld = (h, qs - 2) if qs >= 2 else (h - 1, qs + 6)
            if key_ld in ve_x2_map:
                wait("sync", "VE", ve_x2_map[key_ld])
            op("sync", "dma_start", out=x1t[qs % 2][:, :], in_=x1sp.ap()[h * 8 + qs], inc="LD", amt=16)
            ld_x1 = C["LD"]
            wait("vector", "LD", ld_x1)
            key = (h, qs - 2) if qs >= 2 else (h - 1, qs + 6)
            if key in gp_x2:
                wait("vector", "GP", gp_x2[key])
            op("vector", "scalar_tensor_tensor", out=x2t[qs % 2][:, :], in0=accum[:, qs * D:(qs + 1) * D],
               scalar=den_rc[:, qs:qs + 1], in1=x1t[qs % 2][:, :], op0=OP.mult, op1=OP.add, inc="VE")
            ve_x2 = C["VE"]
            ve_x2_map[(h, qs)] = ve_x2
            gt = h * 8 + qs
            wait("scalar", "VE", ve_x2)
            op("scalar", "activation", out=sq_scr[:, :], in_=x2t[qs % 2][:, :], func=AF.Square,
               accum_out=st_ssq[:, gt:gt + 1], inc="SE")
            se_x2sq = C["SE"]
            op("vector", "tensor_reduce", out=st_sum[:, gt:gt + 1], in_=x2t[qs % 2][:, :],
               axis=AX.X, op=OP.add, inc="VE")
            ve_x2s = C["VE"]
            wait("gpsimd", "VE", ve_x2s)
            wait("gpsimd", "SE", se_x2sq)
            op("gpsimd", "dma_start", out=x2sp.ap()[h * 8 + qs], in_=x2t[qs % 2][:, :], inc="GP", amt=16)
            gp_x2[(h, qs)] = C["GP"]

    psD.close()
    scopeC.close()
    barrier()

    scopeE1 = contextlib.ExitStack()
    ctx["uT"] = scopeE1.enter_context(nc.sbuf_tensor("uT2", [128, NDC * SC], BF16))
    scopeE2 = contextlib.ExitStack()
    ctx["xbuf"] = scopeE2.enter_context(nc.sbuf_tensor("xbuf2", [128, NT * D], F32))
    ctx["wr4"] = [scopeE2.enter_context(nc.sbuf_tensor(f"wr4b_{i}", [128, NDC * D], BF16)) for i in range(3)]
    ctx["u_bf"] = [scopeE2.enter_context(nc.sbuf_tensor(f"u_bf2_{i}", [128, D], BF16)) for i in range(2)]
    psE = contextlib.ExitStack()
    ctx["ps_pair"] = [psE.enter_context(nc.psum_tensor(f"ps_pairb{i}", [128, D], F32)) for i in range(2)]
    psTR4 = contextlib.ExitStack()
    ctx["ps_tr"] = [psTR4.enter_context(nc.psum_tensor(f"ps_trb{i}", [128, D], BF16)) for i in range(2)]
    ctx["ps_sc"] = psTR4.enter_context(nc.psum_tensor("ps_scb", [128, 512], F32))
    psCD4 = contextlib.ExitStack()

    ld_x2 = []
    for t in range(NT):
        op("sync", "dma_start", out=ctx["xbuf"][:, t * D:(t + 1) * D], in_=x2sp.ap()[t], inc="LD", amt=16)
        ld_x2.append(C["LD"])
    wait("scalar", "LD", ld_x2[-1])
    wait("vector", "LD", ld_x2[-1])
    emit_ln(add_half_u=True, stats_ready=True)
    ln4_ve = C["VE"]
    emit_ffn(ln4_ve, [0, 0], [0, 0], out_dma=True,
             ps_trsc_stack=psTR4, ps_cd_stack=psCD4)

    wait("sync", "LD", C["LD"])
    wait("gpsimd", "GP", C["GP"])

    psCD4.close()
    psE.close()
    scopeE2.close()
    scopeE1.close()

    with nc.Block() as block:
        block.sync(lambda e: [f() for f in Q["sync"]])
        block.gpsimd(lambda e: [f() for f in Q["gpsimd"]])
        block.tensor(lambda e: [f() for f in Q["tensor"]])
        block.vector(lambda e: [f() for f in Q["vector"]])
        block.scalar(lambda e: [f() for f in Q["scalar"]])
    est.close()
    return nc


def kernel(L0_attn_x, L0_attn_Wq, L0_attn_bq, L0_attn_Wk, L0_attn_bk, L0_attn_Wv, L0_attn_bv,
           L0_attn_phase_bias, L0_ffn_centroids, L0_ffn_expert_biases, L0_ffn_Wr, L0_ffn_br,
           L0_ffn_Ws, L0_ffn_bs):
    x = np.ascontiguousarray(np.asarray(L0_attn_x, dtype=np.float32).reshape(S, D))
    wr = np.asarray(L0_ffn_Wr, dtype=np.float32)
    ws = np.asarray(L0_ffn_Ws, dtype=np.float32)
    thirds = [ws[0], ws[1], np.asarray(L0_attn_Wq, dtype=np.float32),
              np.asarray(L0_attn_Wk, dtype=np.float32), np.asarray(L0_attn_Wv, dtype=np.float32),
              np.zeros((D, D), np.float32), np.zeros((D, D), np.float32), np.zeros((D, D), np.float32)]
    ident = np.eye(128, dtype=ml_dtypes.bfloat16)
    common = {
        "bq": np.ascontiguousarray(L0_attn_bq, dtype=np.float32),
        "bk": np.ascontiguousarray(L0_attn_bk, dtype=np.float32),
        "bv": np.ascontiguousarray(L0_attn_bv, dtype=np.float32),
        "phb": np.ascontiguousarray(L0_attn_phase_bias, dtype=np.float32),
        "cen": np.ascontiguousarray(L0_ffn_centroids, dtype=np.float32),
        "ebi": np.ascontiguousarray(L0_ffn_expert_biases, dtype=np.float32),
        "brr": np.ascontiguousarray(L0_ffn_br, dtype=np.float32),
        "bs": np.ascontiguousarray(L0_ffn_bs, dtype=np.float32),
        "ident": np.asarray(ident),
    }
    in_maps = []
    for c in range(N_CORES):
        m = dict(common)
        m["x"] = np.ascontiguousarray(x[c * SC:(c + 1) * SC])
        m["wsrc"] = np.ascontiguousarray(np.stack([wr[2 * c], wr[2 * c + 1], thirds[c]], axis=0))
        in_maps.append(m)
    nc = build_nc()
    res = run_bass_kernel_spmd(nc, in_maps, core_ids=list(range(N_CORES)))
    outp = np.concatenate([np.asarray(res.results[c]["out"]) for c in range(N_CORES)], axis=0)
    return outp.reshape(1, S, D).astype(np.float32)


if __name__ == "__main__":
    import reference as _r
    ins = _r.setup_inputs()
    got = kernel(**{k: np.asarray(v) for k, v in ins.items()})
    exp = np.asarray(_r.reference(**ins))
    err = float(np.mean((got - exp) ** 2) / np.mean(exp ** 2))
    print("residual variance ratio:", err)
